# version 18
# baseline (speedup 1.0000x reference)
import numpy as np
import ml_dtypes
from contextlib import ExitStack

import concourse.bass as bass
import concourse.mybir as mybir
from concourse.bass_utils import run_bass_kernel_spmd

F32 = mybir.dt.float32
BF16 = mybir.dt.bfloat16
FP8 = mybir.dt.float8e4
ACT = mybir.ActivationFunctionType
ALU = mybir.AluOpType

NT = 16
D = 1024
FP32_EPS = 1.1920929e-07
LN_EPS = 1e-5
W_EPS = 1e-6


def build_nc():
    nc = bass.Bass("TRN2", target_bir_lowering=False)

    x_in = nc.dram_tensor("x_in", [NT * 128, D], F32, kind="ExternalInput").ap()
    wqkv = nc.dram_tensor("wqkv", [D, 3072], F32, kind="ExternalInput").ap()
    wsw = nc.dram_tensor("wsw", [D, 2048], F32, kind="ExternalInput").ap()
    wo = nc.dram_tensor("wo", [D, D], F32, kind="ExternalInput").ap()
    fk = nc.dram_tensor("fk", [D, D], BF16, kind="ExternalInput").ap()
    bias_bc = nc.dram_tensor("bias_bc", [128, D], BF16, kind="ExternalInput").ap()
    scl = nc.dram_tensor("scl", [128, 2], F32, kind="ExternalInput").ap()
    tri = nc.dram_tensor("tri", [128, 128], BF16, kind="ExternalInput").ap()
    idn = nc.dram_tensor("idn", [128, 128], BF16, kind="ExternalInput").ap()
    ones2 = nc.dram_tensor("ones2", [128, 128], F32, kind="ExternalInput").ap()
    ohw = nc.dram_tensor("ohw", [1, 16], F32, kind="ExternalInput").ap()
    ohk = nc.dram_tensor("ohk", [1, 16], F32, kind="ExternalInput").ap()
    selw = nc.dram_tensor("selw", [8, 1], F32, kind="ExternalInput").ap()
    selk = nc.dram_tensor("selk", [8, 1], F32, kind="ExternalInput").ap()
    e31 = nc.dram_tensor("e31", [32, 128], F32, kind="ExternalInput").ap()
    epsc = nc.dram_tensor("epsc", [128, 4], F32, kind="ExternalInput").ap()
    out = nc.dram_tensor("out", [NT * 128, D], F32, kind="ExternalOutput").ap()
    sp_w = nc.dram_tensor("sp_w", [NT * 128, D], BF16).ap()
    sp_kv = nc.dram_tensor("sp_kv", [NT * 128, D], BF16).ap()
    sp_s1 = nc.dram_tensor("sp_s1", [NT * 128, D], F32).ap()
    sp_y = nc.dram_tensor("sp_y", [NT * 128, D], BF16).ap()
    cc_in = nc.dram_tensor("cc_in", [2, D], F32)
    cc_out = nc.dram_tensor("cc_out", [8, D], F32)
    cc_wi = nc.dram_tensor("cc_wi", [2, 128], F32)
    cc_wo = nc.dram_tensor("cc_wo", [8, 128], F32)

    eo = ExitStack()
    sb = lambda name, shape, dt: eo.enter_context(nc.sbuf_tensor(name, shape, dt))

    wsw_b = sb("wsw_b", [128, 8 * 2048], BF16)
    wo_b = sb("wo_b", [128, 8 * 1024], BF16)
    fk_q = sb("fk_q", [128, 8 * 1024], FP8)
    bias_s = sb("bias_s", [128, D], BF16)
    scl_s = sb("scl_s", [128, 2], F32)
    inv_s = sb("inv_s", [128, 3], F32)
    tri_s = sb("tri_s", [128, 128], BF16)
    idn_s = sb("idn_s", [128, 128], BF16)
    ones_s = sb("ones_s", [128, 128], F32)
    ohw_s = sb("ohw_s", [1, 16], F32)
    ohk_s = sb("ohk_s", [1, 16], F32)
    selw_s = sb("selw_s", [8, 1], F32)
    selk_s = sb("selk_s", [8, 1], F32)
    e31_s = sb("e31_s", [32, 128], F32)
    eps_s = sb("eps_s", [128, 4], F32)
    stat_m = sb("stat_m", [128, NT], F32)
    stat_r = sb("stat_r", [128, NT], F32)
    stat_v = sb("stat_v", [128, NT], F32)
    totw_s = sb("totw_s", [1, D], F32)
    totk_s = sb("totk_s", [1, D], F32)
    tcs = sb("tcs", [8, D], F32)
    offw_s = sb("offw_s", [1, D], F32)
    offk_s = sb("offk_s", [1, D], F32)

    sem = {}
    for name in [
        "dC", "dFK", "dWU", "dWI", "dTC", "dXAa", "dXAb", "dXAc", "tXn", "sXq", "tF2", "sFY", "tUA", "sEvY", "sEvH", "tTY", "tTH", "vXN2", "vYB", "sEB", "vDN", "cS2", "vGB", "dX1a", "dX1b", "dX1c", "dSPa", "dSPb", "dSSa", "dSSb", "dWC",
        "dWVa", "dWVb", "dYSa", "dYSb", "dYLa", "dYLb",
        "dX2a", "dX2b", "dOUTa", "dOUTb", "dCC",
        "vInv", "vStat", "sStat", "vStat2", "sStat2", "vA", "tTr", "sTe",
        "tQK", "tV", "vEvQ", "sEvK", "vBng", "vAgg",
        "sWE", "sS1", "vKV", "chS", "chV", "chG", "chG2", "sFK", "vFK",
        "tTot", "vTot", "cSem", "tSel", "vOff",
        "tCum", "sL2", "sExW", "vExK", "vT2", "sE", "vY", "vKV2",
        "tT2", "tUG", "sSg", "vH", "tO",
        "vS", "vFin", "chS2", "chV2",
    ]:
        sem[name] = eo.enter_context(nc.semaphore(name))

    dX1 = [sem["dX1a"], sem["dX1b"], sem["dX1c"]]
    dXA = [sem["dXAa"], sem["dXAb"], sem["dXAc"]]
    dSP = [sem["dSPa"], sem["dSPb"]]
    dSS = [sem["dSSa"], sem["dSSb"]]
    dWV = [sem["dWVa"], sem["dWVb"]]
    dYS = [sem["dYSa"], sem["dYSb"]]
    dYL = [sem["dYLa"], sem["dYLb"]]
    dX2 = [sem["dX2a"], sem["dX2b"]]
    dOUT = [sem["dOUTa"], sem["dOUTb"]]

    e1 = ExitStack()
    sb1 = lambda name, shape, dt: e1.enter_context(nc.sbuf_tensor(name, shape, dt))
    wqkv_b = sb1("wqkv_b", [128, 8 * 3072], BF16)
    xb1 = [sb1(f"xb1_{j}", [128, D], F32) for j in range(3)]
    abf = [sb1(f"abf{j}", [128, D], BF16) for j in range(2)]
    aT = [sb1(f"aT{j}", [128, D], BF16) for j in range(2)]
    qsb = sb1("qsb", [128, D], F32)
    ksb = sb1("ksb", [128, D], F32)
    wbuf = [sb1(f"wbuf{j}", [128, D], BF16) for j in range(2)]
    kvbuf = [sb1(f"kvbuf{j}", [128, D], BF16) for j in range(2)]
    s1buf = [sb1(f"s1buf{j}", [128, D], F32) for j in range(2)]
    e1buf = sb1("e1buf", [128, D], F32)
    sqjq = sb1("sqjq", [128, D], F32)
    sqjk = sb1("sqjk", [128, D], F32)
    run_w = sb1("run_w", [128, D], F32)
    run_kv = sb1("run_kv", [128, D], F32)
    bnb = sb1("bnb", [128, 12], F32)
    bnq = sb1("bnq", [128, 12], F32)
    bnk = sb1("bnk", [128, 12], F32)
    mvtmp = sb1("mvtmp", [128, 2], F32)
    mvq = sb1("mvq", [128, 2], F32)
    mvk = sb1("mvk", [128, 2], F32)
    ssc = sb1("ssc", [128, 8 * NT], F32)
    vsc = sb1("vsc", [128, 8 * NT], F32)

    ps_q = e1.enter_context(nc.psum_tensor("ps_q", [128, D], F32))
    ps_k = e1.enter_context(nc.psum_tensor("ps_k", [128, D], F32))
    ps_v = e1.enter_context(nc.psum_tensor("ps_v", [128, D], F32))
    psT = e1.enter_context(nc.psum_tensor("psT", [128, D], BF16))

    e1c = {}

    with nc.Block() as block:

        @block.sync
        def _(sync):
            for j in range(3):
                sync.dma_start(out=xb1[j][:], in_=x_in[j * 128:(j + 1) * 128, :]).then_inc(dX1[j], 16)
            sync.dma_start(out=tri_s[:], in_=tri[:]).then_inc(sem["dWU"], 16)
            sync.dma_start(out=idn_s[:], in_=idn[:]).then_inc(sem["dC"], 16)
            sync.dma_start(out=ones_s[:], in_=ones2[:]).then_inc(sem["dC"], 16)
            sync.dma_start(out=selw_s[:], in_=selw[:]).then_inc(sem["dC"], 16)
            sync.dma_start(out=selk_s[:], in_=selk[:]).then_inc(sem["dC"], 16)
            sync.dma_start(out=e31_s[:], in_=e31[:]).then_inc(sem["dC"], 16)
            sync.dma_start(out=scl_s[:], in_=scl[:]).then_inc(sem["dC"], 16)
            sync.dma_start(out=bias_s[:], in_=bias_bc[:]).then_inc(sem["dC"], 16)
            sync.dma_start(out=eps_s[:], in_=epsc[:]).then_inc(sem["dC"], 16)
            for j in range(8):
                sync.dma_start(out=wsw_b[:, j * 1024:(j + 1) * 1024],
                               in_=fk[j * 128:(j + 1) * 128, :]).then_inc(sem["dFK"], 16)
            for i in range(NT):
                if i + 3 < NT:
                    j = i + 3
                    sync.wait_ge(sem["vA"], j - 2)
                    sync.dma_start(out=xb1[j % 3][:],
                                   in_=x_in[j * 128:(j + 1) * 128, :]).then_inc(dX1[j % 3], 16)
                sync.wait_ge(sem["sWE"], i + 1)
                sync.dma_start(out=sp_w[i * 128:(i + 1) * 128, :],
                               in_=wbuf[i % 2][:]).then_inc(dSP[i % 2], 16)
                sync.wait_ge(sem["vKV"], i + 1)
                sync.dma_start(out=sp_kv[i * 128:(i + 1) * 128, :],
                               in_=kvbuf[i % 2][:]).then_inc(dSP[i % 2], 16)
                sync.wait_ge(sem["sS1"], i + 1)
                sync.dma_start(out=sp_s1[i * 128:(i + 1) * 128, :],
                               in_=s1buf[i % 2][:]).then_inc(dSS[i % 2], 16)

        @block.gpsimd
        def _(gpsimd):
            for dc in range(8):
                if dc >= 1:
                    gpsimd.wait_ge(sem["dWC"], 16 * dc)
                gpsimd.dma_start(out=wqkv_b[:, dc * 3072:(dc + 1) * 3072],
                                 in_=wqkv[dc * 128:(dc + 1) * 128, :]).then_inc(sem["dWC"], 16)
            gpsimd.wait_ge(sem["dWU"], 16)
            gpsimd.dma_start(out=cc_wi.ap(), in_=tri_s[0:2, :]).then_inc(sem["dWI"], 16)
            gpsimd.wait_ge(sem["dWI"], 16)
            gpsimd.collective_compute(
                "AllGather", ALU.bypass,
                replica_groups=[[0, 1, 2, 3], [4, 5, 6, 7]],
                ins=[cc_wi.ap().opt()],
                outs=[cc_wo.ap().opt()],
            ).then_inc(sem["cSem"], 1)
            gpsimd.memset(run_w[:], 0.0).then_inc(sem["chG"], 1)
            gpsimd.memset(run_kv[:], 0.0).then_inc(sem["chG2"], 1)
            for i in range(NT):
                p = i % 2
                gpsimd.wait_ge(sem["sWE"], i + 1)
                gpsimd.wait_ge(sem["chG"], i + 1)
                gpsimd.tensor_add(run_w[:], run_w[:], wbuf[p][:]).then_inc(sem["chG"], 1)
                gpsimd.wait_ge(sem["vKV"], i + 1)
                gpsimd.wait_ge(sem["chG2"], i + 1)
                gpsimd.tensor_add(run_kv[:], run_kv[:], kvbuf[p][:]).then_inc(sem["chG2"], 1)

            gpsimd.wait_ge(sem["vTot"], 2)
            gpsimd.dma_start(out=cc_in.ap()[0:1, :], in_=totw_s[:]).then_inc(sem["dCC"], 16)
            gpsimd.dma_start(out=cc_in.ap()[1:2, :], in_=totk_s[:]).then_inc(sem["dCC"], 16)
            gpsimd.wait_ge(sem["dCC"], 32)
            gpsimd.collective_compute(
                "AllGather", ALU.bypass,
                replica_groups=[[0, 1, 2, 3], [4, 5, 6, 7]],
                ins=[cc_in.ap().opt()],
                outs=[cc_out.ap().opt()],
            ).then_inc(sem["cSem"], 1)

        @block.scalar
        def _(scalar):
            chs = [0]

            def cinc(inst):
                chs[0] += 1
                inst.then_inc(sem["chS"], 1)
                return chs[0]

            scalar.wait_ge(sem["dC"], 16 * 8)



            def S_1(j):
                o = 8 * j
                scalar.wait_ge(sem["vStat"], j + 1)
                c = cinc(scalar.activation(out=ssc[:, o:o + 1], in_=stat_v[:, j:j + 1],
                                           func=ACT.Ln, bias=eps_s[:, 0:1]))
                scalar.wait_ge(sem["chS"], c)
                scalar.activation(out=stat_r[:, j:j + 1], in_=ssc[:, o:o + 1], func=ACT.Exp,
                                  scale=-0.5).then_inc(sem["sStat"], 1)

            def S_2(j):
                o = 8 * j
                scalar.wait_ge(sem["vStat2"], j + 1)
                c = cinc(scalar.activation(out=ssc[:, o + 1:o + 2], in_=vsc[:, o + 1:o + 2],
                                           func=ACT.Ln, bias=eps_s[:, 1:2]))
                scalar.wait_ge(sem["chS"], c)
                scalar.activation(out=ssc[:, o + 2:o + 3], in_=ssc[:, o + 1:o + 2], func=ACT.Exp,
                                  scale=-0.5).then_inc(sem["sStat2"], 1)

            def S_evacA(j):
                scalar.wait_ge(sem["tTr"], j + 1)
                scalar.activation(out=aT[j % 2][:], in_=psT[:], func=ACT.Copy).then_inc(sem["sTe"], 1)

            def S_tail(j):
                p = j % 2
                o = 8 * j
                scalar.wait_ge(sem["tQK"], j + 1)
                if j >= 1:
                    scalar.wait_ge(sem["sWE"], j)
                scalar.activation(out=ksb[:], in_=ps_k[:, :], func=ACT.Copy).then_inc(sem["sEvK"], 1)
                scalar.wait_ge(sem["sEvK"], j + 1)
                c0 = cinc(scalar.activation(out=sqjk[:], in_=ksb[:], func=ACT.Square,
                                            accum_out=ssc[:, o + 3:o + 4]))
                scalar.wait_ge(sem["chS"], c0)
                c4 = cinc(scalar.activation(out=ssc[:, o + 4:o + 5], in_=ssc[:, o + 3:o + 4],
                                            func=ACT.Ln, scale=1.0 / D, bias=eps_s[:, 1:2]))
                scalar.wait_ge(sem["chS"], c4)
                c5 = cinc(scalar.activation(out=ssc[:, o + 5:o + 6], in_=ssc[:, o + 4:o + 5],
                                            func=ACT.Exp, scale=-0.5))
                if j >= 2:
                    scalar.wait_ge(sem["chG"], j + 1)
                    scalar.wait_ge(dSP[p], 16 * 2 * (j // 2))
                scalar.wait_ge(sem["chS"], c5)
                scalar.activation(out=wbuf[p][:], in_=ksb[:], func=ACT.Exp,
                                  scale=ssc[:, o + 5:o + 6]).then_inc(sem["sWE"], 1)
                scalar.wait_ge(sem["vEvQ"], j + 1)
                cq = cinc(scalar.activation(out=sqjq[:], in_=qsb[:], func=ACT.Square,
                                            accum_out=ssc[:, o + 6:o + 7]))
                scalar.wait_ge(sem["chS"], cq)
                c1 = cinc(scalar.activation(out=ssc[:, o + 7:o + 8], in_=ssc[:, o + 6:o + 7],
                                            func=ACT.Ln, scale=1.0 / D, bias=eps_s[:, 1:2]))
                scalar.wait_ge(sem["chS"], c1)
                c3 = cinc(scalar.activation(out=ssc[:, o + 3:o + 4], in_=ssc[:, o + 7:o + 8],
                                            func=ACT.Exp, scale=-0.5))
                scalar.wait_ge(sem["chS"], c3)
                cn = cinc(scalar.activation(out=ssc[:, o + 4:o + 5], in_=ssc[:, o + 3:o + 4],
                                            func=ACT.Copy, scale=-1.0))
                scalar.wait_ge(sem["chS"], cn)
                if j >= 1:
                    scalar.wait_ge(sem["sS1"], j)
                c6 = cinc(scalar.activation(out=e1buf[:], in_=qsb[:], func=ACT.Exp,
                                            scale=ssc[:, o + 4:o + 5]))
                if j >= 2:
                    scalar.wait_ge(dSS[p], 16 * (j // 2))
                scalar.wait_ge(sem["chS"], c6)
                scalar.activation(out=s1buf[p][:], in_=e1buf[:], func=ACT.Ln,
                                  bias=eps_s[:, 2:3]).then_inc(sem["sS1"], 1)
                e1c[j] = c6

            S_1(0); S_2(0); S_1(1); S_2(1); S_1(2)
            S_evacA(0)
            for i in range(NT):
                if i == 6:
                    scalar.wait_ge(sem["vInv"], 3)
                    scalar.wait_ge(sem["dFK"], 16 * 8)
                    for j in range(0, 8, 2):
                        scalar.activation(out=fk_q[:, j * 1024:(j + 1) * 1024],
                                          in_=wsw_b[:, j * 1024:(j + 1) * 1024],
                                          func=ACT.Copy, scale=inv_s[:, 1:2]).then_inc(sem["sFK"], 1)
                if i + 1 < NT:
                    S_evacA(i + 1)
                S_tail(i)
                if i + 3 < NT:
                    S_1(i + 3)
                if i + 2 < NT:
                    S_2(i + 2)

        @block.vector
        def _(vector):
            chv = [0]
            agg = [0]
            bng = [0]
            bng2 = [0]
            aggx = {}
            aggqk = {}

            def vinc(inst):
                chv[0] += 1
                inst.then_inc(sem["chV"], 1)
                return chv[0]

            vector.wait_ge(sem["dC"], 16 * 8)
            vector.reciprocal(inv_s[:, 0:1], scl_s[:, 0:1]).then_inc(sem["vInv"], 1)
            vector.reciprocal(inv_s[:, 1:2], scl_s[:, 1:2]).then_inc(sem["vInv"], 1)
            vector.tensor_tensor(inv_s[:, 2:3], scl_s[:, 0:1], scl_s[:, 1:2],
                                 ALU.mult).then_inc(sem["vInv"], 1)


            def V_1(j):
                vector.wait_ge(dX1[j % 3], 16 * (j // 3 + 1))
                if j >= 1:
                    vector.wait_ge(sem["vAgg"], aggx.get(j - 1, 0))
                vector.bn_stats(bnb[:, 0:6], xb1[j % 3][:, 0:512]).then_inc(sem["vBng"], 1)
                vector.bn_stats(bnb[:, 6:12], xb1[j % 3][:, 512:1024]).then_inc(sem["vBng"], 1)
                bng[0] += 2
                vector.wait_ge(sem["vBng"], bng[0])
                if j >= 1:
                    vector.wait_ge(sem["vStat"], j)
                vector.bn_aggr(mvtmp[:], bnb[:]).then_inc(sem["vAgg"], 1)
                agg[0] += 1
                aggx[j] = agg[0]
                vector.wait_ge(sem["vAgg"], agg[0])
                vector.tensor_copy(stat_m[:, j:j + 1], mvtmp[:, 0:1])
                vector.tensor_copy(stat_v[:, j:j + 1], mvtmp[:, 1:2]).then_inc(sem["vStat"], 1)

            def V_2(j):
                o = 8 * j
                vector.wait_ge(sem["sStat"], j + 1)
                c = vinc(vector.tensor_tensor(vsc[:, o:o + 1], stat_r[:, j:j + 1],
                                              stat_r[:, j:j + 1], ALU.mult))
                vector.wait_ge(sem["chV"], c)
                vector.wait_ge(sem["vStat"], j + 1)
                vector.tensor_tensor(vsc[:, o + 1:o + 2], stat_v[:, j:j + 1], vsc[:, o:o + 1],
                                     ALU.mult).then_inc(sem["vStat2"], 1)

            def V_3(j):
                o = 8 * j
                vector.wait_ge(sem["sStat2"], j + 1)
                c = vinc(vector.tensor_tensor(vsc[:, o + 4:o + 5], stat_r[:, j:j + 1],
                                              ssc[:, o + 2:o + 3], ALU.mult))
                if j >= 2:
                    vector.wait_ge(sem["tTr"], j - 1)
                vector.wait_ge(sem["chV"], c)
                vector.tensor_scalar(abf[j % 2][:], xb1[j % 3][:], stat_m[:, j:j + 1],
                                     vsc[:, o + 4:o + 5], ALU.subtract, ALU.mult).then_inc(sem["vA"], 1)

            def V_tail(j):
                p = j % 2
                vector.wait_ge(sem["tQK"], j + 1)
                if j >= 1:
                    vector.wait_ge(sem["chS"], e1c[j - 1])
                vector.tensor_copy(qsb[:], ps_q[:, :]).then_inc(sem["vEvQ"], 1)
                vector.wait_ge(sem["sWE"], j + 1)
                vector.wait_ge(sem["tV"], j + 1)
                if j >= 2:
                    vector.wait_ge(dSP[p], 16 * 2 * (j // 2))
                    vector.wait_ge(sem["chG2"], j + 1)
                vector.tensor_tensor(kvbuf[p][:], wbuf[p][:], ps_v[:, :],
                                     ALU.mult).then_inc(sem["vKV"], 1)

            V_1(0); V_2(0); V_1(1); V_2(1); V_3(0); V_1(2)
            for i in range(NT):
                if i == 7:
                    vector.wait_ge(sem["vInv"], 2)
                    vector.wait_ge(sem["dFK"], 16 * 8)
                    for j in range(1, 8, 2):
                        vector.tensor_scalar(fk_q[:, j * 1024:(j + 1) * 1024],
                                             wsw_b[:, j * 1024:(j + 1) * 1024],
                                             inv_s[:, 1:2], None, ALU.mult).then_inc(sem["vFK"], 1)
                if i + 1 < NT:
                    V_3(i + 1)
                V_tail(i)
                if i + 3 < NT:
                    V_1(i + 3)
                if i + 2 < NT:
                    V_2(i + 2)
            vector.wait_ge(sem["tTot"], 1)
            vector.tensor_copy(totw_s[:], ps_q[0:1, :]).then_inc(sem["vTot"], 1)
            vector.tensor_copy(totk_s[:], ps_k[0:1, :]).then_inc(sem["vTot"], 1)


        @block.tensor
        def _(tensor):
            for i in range(NT):
                p = i % 2
                tensor.wait_ge(sem["vA"], i + 1)
                if i >= 1:
                    tensor.wait_ge(sem["sTe"], i)
                for dc in range(8):
                    inst = tensor.transpose(psT[:, dc * 128:(dc + 1) * 128],
                                            abf[p][:, dc * 128:(dc + 1) * 128], idn_s[:])
                inst.then_inc(sem["tTr"], 1)
                tensor.wait_ge(sem["sTe"], i + 1)
                if i >= 1:
                    tensor.wait_ge(sem["vEvQ"], i)
                    tensor.wait_ge(sem["sEvK"], i)
                for dc in range(8):
                    if i == 0:
                        tensor.wait_ge(sem["dWC"], 16 * (dc + 1))
                    lhs = aT[p][:, dc * 128:(dc + 1) * 128]
                    st = dc == 0
                    sp = dc == 7
                    base = dc * 3072
                    tensor.matmul(ps_q[:, 0:512], lhs, wqkv_b[:, base + 0:base + 512], start=st, stop=sp)
                    tensor.matmul(ps_q[:, 512:1024], lhs, wqkv_b[:, base + 512:base + 1024], start=st, stop=sp)
                    tensor.matmul(ps_k[:, 0:512], lhs, wqkv_b[:, base + 1024:base + 1536], start=st, stop=sp)
                    inst = tensor.matmul(ps_k[:, 512:1024], lhs, wqkv_b[:, base + 1536:base + 2048], start=st, stop=sp)
                inst.then_inc(sem["tQK"], 1)
                if i >= 1:
                    tensor.wait_ge(sem["vKV"], i)
                for dc in range(8):
                    lhs = aT[p][:, dc * 128:(dc + 1) * 128]
                    st = dc == 0
                    sp = dc == 7
                    base = dc * 3072
                    tensor.matmul(ps_v[:, 0:512], lhs, wqkv_b[:, base + 2048:base + 2560], start=st, stop=sp)
                    inst = tensor.matmul(ps_v[:, 512:1024], lhs, wqkv_b[:, base + 2560:base + 3072], start=st, stop=sp)
                inst.then_inc(sem["tV"], 1)
            tensor.wait_ge(sem["chG"], NT + 1)
            tensor.wait_ge(sem["chG2"], NT + 1)
            tensor.wait_ge(sem["sWE"], NT)
            tensor.wait_ge(sem["vKV"], NT)
            tensor.wait_ge(sem["sS1"], NT)
            tensor.matmul(ps_q[0:1, 0:512], ones_s[:, 0:1], run_w[:, 0:512], start=True, stop=True)
            tensor.matmul(ps_q[0:1, 512:1024], ones_s[:, 0:1], run_w[:, 512:1024], start=True, stop=True)
            tensor.matmul(ps_k[0:1, 0:512], ones_s[:, 0:1], run_kv[:, 0:512], start=True, stop=True)
            inst = tensor.matmul(ps_k[0:1, 512:1024], ones_s[:, 0:1], run_kv[:, 512:1024], start=True, stop=True)
            inst.then_inc(sem["tTot"], 1)

    e1.close()
    eg = ExitStack()
    gbuf = [eg.enter_context(nc.sbuf_tensor(f"gbuf{j}", [128, D], BF16)) for j in range(NT)]
    e2 = ExitStack()
    sb2 = lambda name, shape, dt: e2.enter_context(nc.sbuf_tensor(name, shape, dt))
    wkb = [sb2(f"wkb{j}", [128, 2 * D], BF16) for j in range(2)]
    s1b = [sb2(f"s1b{j}", [128, D], F32) for j in range(2)]
    l2b = [sb2(f"l2b{j}", [128, D], F32) for j in range(2)]
    t2b = sb2("t2b", [128, D], F32)
    Eb = sb2("Eb", [128, D], F32)
    ybuf = [sb2(f"ybuf{j}", [128, D], BF16) for j in range(2)]
    kvc = [sb2(f"kvc{j}", [128, D], F32) for j in range(2)]
    ext32 = [sb2("ext32_0", [32, 2 * D], F32)] * 2

    xb2a = [sb2(f"xb2a{j}", [128, D], F32) for j in range(2)]
    xnb2 = sb2("xnb2", [128, D], BF16)
    xqT2 = [sb2(f"xqT2_{j}", [128, D], FP8) for j in range(2)]
    ybb2 = [sb2(f"ybb2_{j}", [128, D], BF16) for j in range(2)]
    yb2b2 = [sb2(f"yb2b2_{j}", [128, D], BF16) for j in range(4)]
    ebf2 = [sb2(f"ebf2_{j}", [128, D], BF16) for j in range(2)]
    denb2 = [sb2(f"denb2_{j}", [128, D], BF16) for j in range(2)]
    sgc2 = [sb2("sgc2_0", [128, D], F32)] * 2
    sgb2b2 = [sb2("sgb2b2_0", [128, D], BF16)] * 2

    PW = e2.enter_context(nc.psum_tensor("PW", [128, D], F32))
    PK = e2.enter_context(nc.psum_tensor("PK", [128, D], F32))
    psT2 = [e2.enter_context(nc.psum_tensor(f"psT2_{j}", [128, D], BF16)) for j in range(2)]
    OC2 = e2.enter_context(nc.psum_tensor("OC2", [128, D], F32))

    with nc.Block() as block:

        @block.gpsimd
        def _(gpsimd):
            gpsimd.wait_ge(sem["sFK"], 4)
            gpsimd.wait_ge(sem["vFK"], 4)
            for dc in range(8):
                gpsimd.dma_start(out=wsw_b[:, dc * 2048:(dc + 1) * 2048],
                                 in_=wsw[dc * 128:(dc + 1) * 128, :]).then_inc(sem["dWC"], 16)
            for dc in range(8):
                gpsimd.dma_start(out=wo_b[:, dc * 1024:(dc + 1) * 1024],
                                 in_=wo[dc * 128:(dc + 1) * 128, :]).then_inc(sem["dWC"], 16)

        @block.sync
        def _(sync):
            sync.wait_ge(dSP[0], 16 * 16)
            sync.wait_ge(dSP[1], 16 * 16)
            sync.wait_ge(dSS[0], 16 * 8)
            sync.wait_ge(dSS[1], 16 * 8)
            for i in range(2):
                sync.dma_start(out=wkb[i][:, 0:D], in_=sp_w[i * 128:(i + 1) * 128, :]).then_inc(dWV[i], 16)
                sync.dma_start(out=wkb[i][:, D:2 * D], in_=sp_kv[i * 128:(i + 1) * 128, :]).then_inc(dWV[i], 16)
                sync.dma_start(out=s1b[i][:], in_=sp_s1[i * 128:(i + 1) * 128, :]).then_inc(dWV[i], 16)
            sync.wait_ge(sem["cSem"], 2)
            sync.dma_start(out=tcs[:], in_=cc_out.ap()).then_inc(sem["dTC"], 16)
            for i in range(NT):
                p = i % 2
                if i + 2 < NT:
                    sync.wait_ge(sem["tCum"], i + 1)
                    sync.wait_ge(sem["vT2"], i + 1)
                    sync.dma_start(out=wkb[p][:, 0:D], in_=sp_w[(i + 2) * 128:(i + 3) * 128, :]).then_inc(dWV[p], 16)
                    sync.dma_start(out=wkb[p][:, D:2 * D], in_=sp_kv[(i + 2) * 128:(i + 3) * 128, :]).then_inc(dWV[p], 16)
                    sync.dma_start(out=s1b[p][:], in_=sp_s1[(i + 2) * 128:(i + 3) * 128, :]).then_inc(dWV[p], 16)
                sync.wait_ge(sem["vY"], i + 1)
                sync.dma_start(out=sp_y[i * 128:(i + 1) * 128, :], in_=ybuf[p][:]).then_inc(dYS[p], 16)
            sync.wait_ge(dYS[0], 16 * 8)
            sync.wait_ge(dYS[1], 16 * 8)

        @block.tensor
        def _(tensor):
            tensor.wait_ge(sem["sFK"], 4)
            tensor.wait_ge(sem["vFK"], 4)
            for ii in range(NT + 3):
                if ii < NT:
                    tensor.wait_ge(sem["vXN2"], ii + 1)
                    if ii >= 2:
                        tensor.wait_ge(sem["sXq"], ii - 1)
                    for dc in range(8):
                        inst = tensor.transpose(psT2[ii % 2][:, dc * 128:(dc + 1) * 128],
                                                xnb2[:, dc * 128:(dc + 1) * 128], idn_s[:])
                    inst.then_inc(sem["tXn"], 1)
                if ii == 3:
                    tensor.wait_ge(sem["dTC"], 16)
                    tensor.matmul(PW[0:1, 0:512], selw_s[:, :], tcs[:, 0:512], start=True, stop=True)
                    tensor.matmul(PW[0:1, 512:1024], selw_s[:, :], tcs[:, 512:1024], start=True, stop=True)
                    tensor.matmul(PK[0:1, 0:512], selk_s[:, :], tcs[:, 0:512], start=True, stop=True)
                    inst = tensor.matmul(PK[0:1, 512:1024], selk_s[:, :], tcs[:, 512:1024], start=True, stop=True)
                    inst.then_inc(sem["tSel"], 1)
                if ii >= 3:
                    i = ii - 3
                    p = i % 2
                    tensor.wait_ge(dWV[p], 16 * 3 * (i // 2 + 1))
                    if i == 0:
                        tensor.wait_ge(sem["vOff"], 2)
                    if i >= 1:
                        tensor.wait_ge(sem["sL2"], i)
                        tensor.wait_ge(sem["sExW"], i)
                        tensor.wait_ge(sem["vKV2"], i)
                    tensor.matmul(PW[:, 0:512], tri_s[:], wkb[p][:, 0:512], start=True, stop=False)
                    tensor.matmul(PW[:, 512:1024], tri_s[:], wkb[p][:, 512:1024], start=True, stop=False)
                    tensor.matmul(PK[:, 0:512], tri_s[:], wkb[p][:, D:D + 512], start=True, stop=False)
                    tensor.matmul(PK[:, 512:1024], tri_s[:], wkb[p][:, D + 512:2 * D], start=True, stop=False)
                    if i == 0:
                        tensor.matmul(PW[:, 0:512], ones_s[0:1, :], offw_s[0:1, 0:512], start=False, stop=True)
                        tensor.matmul(PW[:, 512:1024], ones_s[0:1, :], offw_s[0:1, 512:1024], start=False, stop=True)
                        tensor.matmul(PK[:, 0:512], ones_s[0:1, :], offk_s[0:1, 0:512], start=False, stop=True)
                        inst = tensor.matmul(PK[:, 512:1024], ones_s[0:1, :], offk_s[0:1, 512:1024], start=False, stop=True)
                    else:
                        q = 1 - p
                        tensor.wait_ge(sem["sExW"], i)
                        tensor.wait_ge(sem["vExK"], i)
                        tensor.matmul(PW[:, 0:512], e31_s[:, :], ext32[q][:, 0:512], start=False, stop=True)
                        tensor.matmul(PW[:, 512:1024], e31_s[:, :], ext32[q][:, 512:1024], start=False, stop=True)
                        tensor.matmul(PK[:, 0:512], e31_s[:, :], ext32[q][:, D:D + 512], start=False, stop=True)
                        inst = tensor.matmul(PK[:, 512:1024], e31_s[:, :], ext32[q][:, D + 512:2 * D], start=False, stop=True)
                    inst.then_inc(sem["tCum"], 1)
                if ii < NT:
                        tensor.wait_ge(sem["sXq"], ii + 1)
                        if ii >= 1:
                            tensor.wait_ge(sem["sFY"], ii)
                        for dc in range(8):
                            lhs = xqT2[ii % 2][:, dc * 128:(dc + 1) * 128]
                            st = dc == 0
                            sp = dc == 7
                            base = dc * 1024
                            tensor.matmul(OC2[:, 0:512], lhs, fk_q[:, base + 0:base + 512], start=st, stop=sp)
                            inst = tensor.matmul(OC2[:, 512:1024], lhs, fk_q[:, base + 512:base + 1024], start=st, stop=sp)
                        inst.then_inc(sem["tF2"], 1)

        @block.scalar
        def _(scalar):
            cs2 = [0]

            def S_head(j):
                scalar.wait_ge(sem["tCum"], j + 1)
                if j >= 2:
                    scalar.wait_ge(sem["vT2"], j - 1)
                scalar.activation(out=l2b[j % 2][:], in_=PW[:, :], func=ACT.Ln,
                                  bias=eps_s[:, 3:4]).then_inc(sem["sL2"], 1)
                scalar.activation(out=ext32[j % 2][:, 0:D], in_=PW[96:128, :],
                                  func=ACT.Copy).then_inc(sem["sExW"], 1)

            def S_E(j):
                scalar.wait_ge(sem["vT2"], j + 1)
                if j >= 1:
                    scalar.wait_ge(sem["vY"], j)
                scalar.activation(out=Eb[:], in_=t2b[:], func=ACT.Exp,
                                  scale=-1.0).then_inc(sem["sE"], 1)

            scalar.wait_ge(dSP[0], 16 * 16)
            scalar.wait_ge(dSP[1], 16 * 16)
            scalar.wait_ge(dSS[0], 16 * 8)
            scalar.wait_ge(dSS[1], 16 * 8)
            scalar.dma_start(out=xb2a[0][:], in_=x_in[0:128, :]).then_inc(dXA[0], 16)
            scalar.dma_start(out=xb2a[1][:], in_=x_in[128:256, :]).then_inc(dXA[1], 16)
            for ii in range(NT + 6):
                if ii + 2 < NT:
                    scalar.wait_ge(sem["vXN2"], ii + 1)
                    scalar.dma_start(out=xb2a[(ii + 2) % 2][:],
                                     in_=x_in[(ii + 2) * 128:(ii + 3) * 128, :]).then_inc(dXA[(ii + 2) % 2], 16)
                if ii < NT:
                    scalar.wait_ge(sem["tXn"], ii + 1)
                    if ii >= 2:
                        scalar.wait_ge(sem["tF2"], ii - 1)
                    scalar.activation(out=xqT2[ii % 2][:], in_=psT2[ii % 2][:], func=ACT.Copy,
                                      scale=inv_s[:, 0:1]).then_inc(sem["sXq"], 1)
                j = ii - 1
                if 0 <= j < NT:
                    scalar.wait_ge(sem["tF2"], j + 1)
                    if j >= 2:
                        scalar.wait_ge(sem["vYB"], j - 1)
                    scalar.activation(out=ybb2[j % 2][:], in_=OC2[:, :], func=ACT.Copy,
                                      scale=inv_s[:, 2:3]).then_inc(sem["sFY"], 1)
                j = ii - 3
                if 0 <= j < NT:
                    scalar.wait_ge(sem["vYB"], j + 1)
                    if j >= 2:
                        scalar.wait_ge(sem["vDN"], j - 1)
                    scalar.activation(out=ebf2[j % 2][:], in_=yb2b2[j % 4][:], func=ACT.Exp,
                                      scale=-1.0).then_inc(sem["sEB"], 1)
                j = ii - 5
                if 0 <= j < NT:
                    scalar.wait_ge(sem["vDN"], j + 1)
                    scalar.wait_ge(sem["cS2"], 2 * j)
                    cs2[0] += 1
                    scalar.activation(out=sgc2[j % 2][:], in_=denb2[j % 2][:], func=ACT.Ln,
                                      bias=0.0).then_inc(sem["cS2"], 1)
                j = ii - 3
                if 0 <= j < NT:
                    S_head(j)
                    S_E(j)
                j = ii - 5
                if 0 <= j < NT:
                    scalar.wait_ge(sem["cS2"], 2 * j + 1)
                    if j >= 1:
                        scalar.wait_ge(sem["vGB"], j)
                    cs2[0] += 1
                    scalar.activation(out=sgb2b2[j % 2][:], in_=sgc2[j % 2][:], func=ACT.Exp,
                                      scale=-1.0).then_inc(sem["cS2"], 1)

        @block.vector
        def _(vector):
            cv2 = [0]

            def V_head(j):
                pj = j % 2
                if j == 0:
                    vector.wait_ge(sem["tSel"], 1)
                    vector.tensor_copy(offw_s[:], PW[0:1, :]).then_inc(sem["vOff"], 1)
                    vector.tensor_copy(offk_s[:], PK[0:1, :]).then_inc(sem["vOff"], 1)
                vector.wait_ge(sem["tCum"], j + 1)
                if j >= 2:
                    vector.wait_ge(sem["vY"], j - 1)
                vector.tensor_copy(kvc[pj][:], PK[:, :]).then_inc(sem["vKV2"], 1)
                vector.wait_ge(sem["vKV2"], j + 1)
                vector.tensor_copy(ext32[pj][:, D:2 * D], kvc[pj][96:128, :]).then_inc(sem["vExK"], 1)
                vector.wait_ge(sem["sL2"], j + 1)
                if j >= 1:
                    vector.wait_ge(sem["sE"], j)
                vector.tensor_tensor(t2b[:], s1b[pj][:], l2b[pj][:], ALU.add).then_inc(sem["vT2"], 1)

            def V_y(j):
                pj = j % 2
                if j >= 2:
                    vector.wait_ge(dYS[pj], 16 * (j // 2))
                vector.wait_ge(sem["sE"], j + 1)
                vector.tensor_tensor(ybuf[pj][:], kvc[pj][:], Eb[:], ALU.mult).then_inc(sem["vY"], 1)

            for ii in range(NT + 6):
                if ii < NT:
                    vector.wait_ge(dXA[ii % 2], 16 * (ii // 2 + 1))
                    if ii >= 1:
                        vector.wait_ge(sem["tXn"], ii)
                    vector.tensor_scalar(xnb2[:], xb2a[ii % 2][:], stat_m[:, ii:ii + 1],
                                         stat_r[:, ii:ii + 1], ALU.subtract,
                                         ALU.mult).then_inc(sem["vXN2"], 1)
                j = ii - 6
                if 0 <= j < NT:
                    vector.wait_ge(sem["cS2"], 2 * j + 2)
                    vector.tensor_tensor(gbuf[j][:], yb2b2[j % 4][:], sgb2b2[j % 2][:],
                                         ALU.mult).then_inc(sem["vGB"], 1)
                j = ii - 4
                if 0 <= j < NT:
                    vector.wait_ge(sem["sEB"], j + 1)
                    if j >= 2:
                        vector.wait_ge(sem["cS2"], 2 * (j - 2) + 1)
                    vector.tensor_scalar(denb2[j % 2][:], ebf2[j % 2][:], 1.0, None,
                                         ALU.add).then_inc(sem["vDN"], 1)
                j = ii - 2
                if 0 <= j < NT:
                    vector.wait_ge(sem["sFY"], j + 1)
                    if j >= 4:
                        vector.wait_ge(sem["vGB"], j - 3)
                    if j >= 2:
                        vector.wait_ge(sem["sEB"], j - 1)
                    vector.tensor_tensor(yb2b2[j % 4][:], ybb2[j % 2][:], bias_s[:],
                                         ALU.add).then_inc(sem["vYB"], 1)
                j = ii - 3
                if 0 <= j < NT:
                    V_head(j)
                    V_y(j)

    e2.close()
    e3 = ExitStack()
    sb3 = lambda name, shape, dt: e3.enter_context(nc.sbuf_tensor(name, shape, dt))
    yb2t = [sb3(f"yb2t{j}", [128, D], BF16) for j in range(2)]
    xb2 = [sb3(f"xb2_{j}", [128, D], F32) for j in range(2)]
    yT_s = [sb3(f"yT_s{j}", [128, D], BF16) for j in range(2)]
    thb = sb3("thb", [128, D], F32)
    sgb = sb3("sgb", [128, D], F32)
    sgb2 = sb3("sgb2", [128, D], F32)
    pb = sb3("pb", [128, D], F32)
    hb = sb3("hb", [128, D], BF16)
    hT_s = sb3("hT_s", [128, D], BF16)
    xnf = sb3("xnf", [128, D], F32)
    sbf = sb3("sbf", [128, D], F32)
    sg3b = sb3("sg3b", [128, D], F32)
    outb = [sb3(f"outb{j}", [128, D], F32) for j in range(2)]

    PT = [e3.enter_context(nc.psum_tensor(f"PT{j}", [128, D], BF16)) for j in range(2)]
    UA = [e3.enter_context(nc.psum_tensor(f"UA{j}", [128, D], F32)) for j in range(2)]
    GB = e3.enter_context(nc.psum_tensor("GB", [128, D], F32))

    with nc.Block() as block:

        @block.sync
        def _(sync):
            for i in range(2):
                sync.dma_start(out=yb2t[i][:], in_=sp_y[i * 128:(i + 1) * 128, :]).then_inc(dYL[i], 16)
                sync.dma_start(out=xb2[i][:], in_=x_in[i * 128:(i + 1) * 128, :]).then_inc(dX2[i], 16)
            for i in range(NT):
                p = i % 2
                if i + 2 < NT:
                    sync.wait_ge(sem["tUG"], i + 1)
                    sync.dma_start(out=yb2t[p][:], in_=sp_y[(i + 2) * 128:(i + 3) * 128, :]).then_inc(dYL[p], 16)
                    sync.wait_ge(sem["vFin"], i + 1)
                    sync.dma_start(out=xb2[p][:], in_=x_in[(i + 2) * 128:(i + 3) * 128, :]).then_inc(dX2[p], 16)
                sync.wait_ge(sem["vFin"], i + 1)
                sync.dma_start(out=out[i * 128:(i + 1) * 128, :], in_=outb[p][:]).then_inc(dOUT[p], 16)
            sync.wait_ge(dOUT[0], 16 * 8)
            sync.wait_ge(dOUT[1], 16 * 8)

        @block.tensor
        def _(tensor):
            tensor.wait_ge(sem["dWC"], 16 * 24)
            for ii in range(NT + 1):
                if ii < NT:
                    i = ii
                    p = i % 2
                    tensor.wait_ge(dYL[p], 16 * (i // 2 + 1))
                    if i >= 1:
                        tensor.wait_ge(sem["sEvY"], i)
                    for dc in range(8):
                        inst = tensor.transpose(PT[0][:, dc * 128:(dc + 1) * 128],
                                                yb2t[p][:, dc * 128:(dc + 1) * 128], idn_s[:])
                    inst.then_inc(sem["tTY"], 1)
                    tensor.wait_ge(sem["sEvY"], i + 1)
                    if i >= 2:
                        tensor.wait_ge(sem["vH"], i - 1)
                    for dc in range(8):
                        lhs = yT_s[i % 2][:, dc * 128:(dc + 1) * 128]
                        st = dc == 0
                        sp = dc == 7
                        base = dc * 2048
                        tensor.matmul(UA[i % 2][:, 0:512], lhs, wsw_b[:, base + 0:base + 512], start=st, stop=sp)
                        inst = tensor.matmul(UA[i % 2][:, 512:1024], lhs, wsw_b[:, base + 512:base + 1024], start=st, stop=sp)
                    inst.then_inc(sem["tUA"], 1)
                if ii >= 1:
                    i = ii - 1
                    p = i % 2
                    if i >= 1:
                        tensor.wait_ge(sem["vS"], i)
                    for dc in range(8):
                        lhs = yT_s[i % 2][:, dc * 128:(dc + 1) * 128]
                        st = dc == 0
                        sp = dc == 7
                        base = dc * 2048
                        tensor.matmul(GB[:, 0:512], lhs, wsw_b[:, base + 1024:base + 1536], start=st, stop=sp)
                        inst = tensor.matmul(GB[:, 512:1024], lhs, wsw_b[:, base + 1536:base + 2048], start=st, stop=sp)
                    inst.then_inc(sem["tUG"], 1)
                    tensor.wait_ge(sem["vH"], i + 1)
                    if i >= 1:
                        tensor.wait_ge(sem["sEvH"], i)
                    for dc in range(8):
                        inst = tensor.transpose(PT[1][:, dc * 128:(dc + 1) * 128],
                                                hb[:, dc * 128:(dc + 1) * 128], idn_s[:])
                    inst.then_inc(sem["tTH"], 1)
                    tensor.wait_ge(sem["sEvH"], i + 1)
                    tensor.wait_ge(sem["sSg"], i + 1)
                    for dc in range(8):
                        lhs = hT_s[:, dc * 128:(dc + 1) * 128]
                        st = dc == 0
                        sp = dc == 7
                        base = dc * 1024
                        tensor.matmul(GB[:, 0:512], lhs, wo_b[:, base + 0:base + 512], start=st, stop=sp)
                        inst = tensor.matmul(GB[:, 512:1024], lhs, wo_b[:, base + 512:base + 1024], start=st, stop=sp)
                    inst.then_inc(sem["tO"], 1)

        @block.scalar
        def _(scalar):
            for ii in range(NT + 1):
                if ii < NT:
                    i = ii
                    scalar.wait_ge(sem["tTY"], i + 1)
                    if i >= 2:
                        scalar.wait_ge(sem["tUG"], i - 1)
                    scalar.activation(out=yT_s[i % 2][:], in_=PT[0][:], func=ACT.Copy).then_inc(sem["sEvY"], 1)
                if ii >= 1:
                    i = ii - 1
                    scalar.wait_ge(sem["tUG"], i + 1)
                    scalar.activation(out=thb[:], in_=GB[:, :], func=ACT.Exp,
                                      scale=-1.0).then_inc(sem["chS2"], 1)
                    scalar.wait_ge(sem["chS2"], 2 * i + 1)
                    scalar.activation(out=sgb[:], in_=thb[:], func=ACT.Ln,
                                      bias=eps_s[:, 2:3]).then_inc(sem["chS2"], 1)
                    scalar.wait_ge(sem["chS2"], 2 * i + 2)
                    scalar.activation(out=sgb2[:], in_=sgb[:], func=ACT.Exp,
                                      scale=-1.0).then_inc(sem["sSg"], 1)
                    scalar.wait_ge(sem["tTH"], i + 1)
                    scalar.activation(out=hT_s[:], in_=PT[1][:], func=ACT.Copy).then_inc(sem["sEvH"], 1)

        @block.vector
        def _(vector):
            for i in range(NT):
                p = i % 2
                vector.wait_ge(dX2[p], 16 * (i // 2 + 1))
                if i >= 1:
                    vector.wait_ge(sem["vS"], i)
                vector.tensor_scalar(xnf[:], xb2[p][:], stat_m[:, i:i + 1], stat_r[:, i:i + 1],
                                     ALU.subtract, ALU.mult).then_inc(sem["chV2"], 1)
                vector.wait_ge(sem["sSg"], i + 1)
                if i >= 1:
                    vector.wait_ge(sem["vH"], i)
                vector.tensor_tensor(pb[:], GB[:, :], sgb2[:], ALU.mult).then_inc(sem["chV2"], 1)
                vector.wait_ge(sem["chV2"], 3 * i + 2)
                vector.wait_ge(sem["tUA"], i + 1)
                if i >= 1:
                    vector.wait_ge(sem["tTH"], i)
                vector.tensor_tensor(hb[:], UA[i % 2][:, :], pb[:], ALU.mult).then_inc(sem["vH"], 1)
                vector.wait_ge(sem["tO"], i + 1)
                if i >= 1:
                    vector.wait_ge(sem["vFin"], i)
                vector.tensor_tensor(sbf[:], xnf[:], GB[:, :], ALU.add).then_inc(sem["vS"], 1)
                vector.wait_ge(sem["vS"], i + 1)
                vector.tensor_tensor(sg3b[:], sbf[:], gbuf[i][:], ALU.mult).then_inc(sem["chV2"], 1)
                if i >= 2:
                    vector.wait_ge(dOUT[p], 16 * (i // 2))
                vector.wait_ge(sem["chV2"], 3 * i + 3)
                vector.tensor_tensor(outb[p][:], xb2[p][:], sg3b[:], ALU.add).then_inc(sem["vFin"], 1)

    e3.close()
    eg.close()
    eo.close()
    return nc


_NC_CACHE = {}


def _get_nc():
    if "nc" not in _NC_CACHE:
        _NC_CACHE["nc"] = build_nc()
    return _NC_CACHE["nc"]


def kernel(L0_attn_x, L0_attn_w_qkv, L0_attn_w_swiglu, L0_attn_w_out,
           L0_ffn_kernel, L0_ffn_input_scale, L0_ffn_kernel_scale, L0_ffn_bias):
    x = np.asarray(L0_attn_x, dtype=np.float32)
    wqkv_t = np.ascontiguousarray(np.asarray(L0_attn_w_qkv, dtype=np.float32).T)
    wsw_t = np.ascontiguousarray(np.asarray(L0_attn_w_swiglu, dtype=np.float32).T)
    wo_t = np.ascontiguousarray(np.asarray(L0_attn_w_out, dtype=np.float32).T)
    fk = np.ascontiguousarray(np.asarray(L0_ffn_kernel).astype(ml_dtypes.bfloat16, copy=False))
    bias = np.asarray(L0_ffn_bias).astype(ml_dtypes.bfloat16, copy=False)
    bias_bc = np.ascontiguousarray(np.broadcast_to(bias[None, :], (128, D)))
    s_in = np.asarray(L0_ffn_input_scale, dtype=np.float32).reshape(1)[0]
    s_k = np.asarray(L0_ffn_kernel_scale, dtype=np.float32).reshape(1)[0]
    scl = np.ascontiguousarray(np.broadcast_to(np.array([[s_in, s_k]], dtype=np.float32), (128, 2)))

    tri = np.ascontiguousarray(np.triu(np.ones((128, 128), dtype=np.float32)).astype(ml_dtypes.bfloat16))
    idn = np.ascontiguousarray(np.eye(128, dtype=np.float32).astype(ml_dtypes.bfloat16))
    ones2 = np.ones((128, 128), dtype=np.float32)
    e31_h = np.zeros((32, 128), dtype=np.float32)
    e31_h[31, :] = 1.0
    epsc = np.ascontiguousarray(np.broadcast_to(
        np.array([[LN_EPS, FP32_EPS, 1.0, W_EPS]], dtype=np.float32), (128, 4)))

    in_maps = []
    for c in range(8):
        b, q = c // 4, c % 4
        xs = np.ascontiguousarray(x[b, q * 2048:(q + 1) * 2048, :])
        ohw_h = np.zeros((1, 16), dtype=np.float32)
        ohw_h[0, 2 * c] = 1.0
        ohk_h = np.zeros((1, 16), dtype=np.float32)
        ohk_h[0, 2 * c + 1] = 1.0
        selw_h = np.zeros((8, 1), dtype=np.float32)
        selk_h = np.zeros((8, 1), dtype=np.float32)
        for r in range(q):
            selw_h[2 * r, 0] = 1.0
            selk_h[2 * r + 1, 0] = 1.0
        in_maps.append({
            "x_in": xs, "wqkv": wqkv_t, "wsw": wsw_t, "wo": wo_t, "fk": fk,
            "bias_bc": bias_bc, "scl": scl, "tri": tri, "idn": idn,
            "ones2": ones2, "ohw": ohw_h, "ohk": ohk_h, "selw": selw_h,
            "selk": selk_h, "e31": e31_h, "epsc": epsc,
        })

    nc = _get_nc()
    res = run_bass_kernel_spmd(nc, in_maps, core_ids=list(range(8)))
    out = np.empty((2, 8192, D), dtype=np.float32)
    for c in range(8):
        b, q = c // 4, c % 4
        out[b, q * 2048:(q + 1) * 2048, :] = res.results[c]["out"]
    return out


# version 19 (final)
# speedup vs baseline: 1.0782x; 1.0782x over previous
import numpy as np
import ml_dtypes
from contextlib import ExitStack

import concourse.bass as bass
import concourse.mybir as mybir
from concourse.bass_utils import run_bass_kernel_spmd

F32 = mybir.dt.float32
BF16 = mybir.dt.bfloat16
FP8 = mybir.dt.float8e4
ACT = mybir.ActivationFunctionType
ALU = mybir.AluOpType

NT = 16
D = 1024
FP32_EPS = 1.1920929e-07
LN_EPS = 1e-5
W_EPS = 1e-6


def build_nc():
    nc = bass.Bass("TRN2", target_bir_lowering=False)

    x_in = nc.dram_tensor("x_in", [NT * 128, D], F32, kind="ExternalInput").ap()
    wqkv = nc.dram_tensor("wqkv", [D, 3072], F32, kind="ExternalInput").ap()
    wsw = nc.dram_tensor("wsw", [D, 2048], F32, kind="ExternalInput").ap()
    wo = nc.dram_tensor("wo", [D, D], F32, kind="ExternalInput").ap()
    fk = nc.dram_tensor("fk", [D, D], BF16, kind="ExternalInput").ap()
    bias_bc = nc.dram_tensor("bias_bc", [128, D], BF16, kind="ExternalInput").ap()
    scl = nc.dram_tensor("scl", [128, 2], F32, kind="ExternalInput").ap()
    tri = nc.dram_tensor("tri", [128, 128], BF16, kind="ExternalInput").ap()
    idn = nc.dram_tensor("idn", [128, 128], BF16, kind="ExternalInput").ap()
    ones2 = nc.dram_tensor("ones2", [128, 128], F32, kind="ExternalInput").ap()
    ohw = nc.dram_tensor("ohw", [1, 16], F32, kind="ExternalInput").ap()
    ohk = nc.dram_tensor("ohk", [1, 16], F32, kind="ExternalInput").ap()
    selw = nc.dram_tensor("selw", [8, 1], F32, kind="ExternalInput").ap()
    selk = nc.dram_tensor("selk", [8, 1], F32, kind="ExternalInput").ap()
    e31 = nc.dram_tensor("e31", [32, 128], F32, kind="ExternalInput").ap()
    epsc = nc.dram_tensor("epsc", [128, 4], F32, kind="ExternalInput").ap()
    out = nc.dram_tensor("out", [NT * 128, D], F32, kind="ExternalOutput").ap()
    sp_w = nc.dram_tensor("sp_w", [NT * 128, D], BF16).ap()
    sp_kv = nc.dram_tensor("sp_kv", [NT * 128, D], BF16).ap()
    sp_s1 = nc.dram_tensor("sp_s1", [NT * 128, D], F32).ap()
    sp_y = nc.dram_tensor("sp_y", [NT * 128, D], BF16).ap()
    cc_in = nc.dram_tensor("cc_in", [2, D], F32)
    cc_out = nc.dram_tensor("cc_out", [8, D], F32)
    cc_wi = nc.dram_tensor("cc_wi", [2, 128], F32)
    cc_wo = nc.dram_tensor("cc_wo", [8, 128], F32)

    eo = ExitStack()
    sb = lambda name, shape, dt: eo.enter_context(nc.sbuf_tensor(name, shape, dt))

    wsw_b = sb("wsw_b", [128, 8 * 2048], BF16)
    wo_b = sb("wo_b", [128, 8 * 1024], BF16)
    fk_q = sb("fk_q", [128, 8 * 1024], FP8)
    bias_s = sb("bias_s", [128, D], BF16)
    scl_s = sb("scl_s", [128, 2], F32)
    inv_s = sb("inv_s", [128, 3], F32)
    tri_s = sb("tri_s", [128, 128], BF16)
    idn_s = sb("idn_s", [128, 128], BF16)
    ones_s = sb("ones_s", [128, 128], F32)
    ohw_s = sb("ohw_s", [1, 16], F32)
    ohk_s = sb("ohk_s", [1, 16], F32)
    selw_s = sb("selw_s", [8, 1], F32)
    selk_s = sb("selk_s", [8, 1], F32)
    e31_s = sb("e31_s", [32, 128], F32)
    eps_s = sb("eps_s", [128, 4], F32)
    stat_m = sb("stat_m", [128, NT], F32)
    stat_r = sb("stat_r", [128, NT], F32)
    stat_v = sb("stat_v", [128, NT], F32)
    totw_s = sb("totw_s", [1, D], F32)
    totk_s = sb("totk_s", [1, D], F32)
    tcs = sb("tcs", [8, D], F32)
    offw_s = sb("offw_s", [1, D], F32)
    offk_s = sb("offk_s", [1, D], F32)

    sem = {}
    for name in [
        "dC", "dFK", "dWU", "dWI", "dTC", "dXAa", "dXAb", "dXAc", "tXn", "sXq", "tF2", "sFY", "tUA", "sEvY", "sEvH", "tTY", "tTH", "vXN2", "vYB", "sEB", "vDN", "cS2", "vGB", "dX1a", "dX1b", "dX1c", "dSPa", "dSPb", "dSSa", "dSSb", "dWC",
        "dWVa", "dWVb", "dYSa", "dYSb", "dYLa", "dYLb",
        "dX2a", "dX2b", "dOUTa", "dOUTb", "dCC",
        "vInv", "vStat", "sStat", "vStat2", "sStat2", "vA", "tTr", "sTe",
        "tQK", "tV", "vEvQ", "sEvK", "vBng", "vAgg",
        "sWE", "sS1", "vKV", "chS", "chV", "chG", "chG2", "sFK", "vFK",
        "tTot", "vTot", "cSem", "tSel", "vOff",
        "tCum", "sL2", "sExW", "vExK", "vT2", "sE", "vY", "vKV2",
        "tT2", "tUG", "sSg", "vH", "tO",
        "vS", "vFin", "chS2", "chV2",
    ]:
        sem[name] = eo.enter_context(nc.semaphore(name))

    dX1 = [sem["dX1a"], sem["dX1b"], sem["dX1c"]]
    dXA = [sem["dXAa"], sem["dXAb"], sem["dXAc"]]
    dSP = [sem["dSPa"], sem["dSPb"]]
    dSS = [sem["dSSa"], sem["dSSb"]]
    dWV = [sem["dWVa"], sem["dWVb"]]
    dYS = [sem["dYSa"], sem["dYSb"]]
    dYL = [sem["dYLa"], sem["dYLb"]]
    dX2 = [sem["dX2a"], sem["dX2b"]]
    dOUT = [sem["dOUTa"], sem["dOUTb"]]

    e1 = ExitStack()
    sb1 = lambda name, shape, dt: e1.enter_context(nc.sbuf_tensor(name, shape, dt))
    wqkv_b = sb1("wqkv_b", [128, 8 * 3072], BF16)
    xb1 = [sb1(f"xb1_{j}", [128, D], F32) for j in range(3)]
    abf = [sb1(f"abf{j}", [128, D], BF16) for j in range(2)]
    aT = [sb1(f"aT{j}", [128, D], BF16) for j in range(2)]
    qsb = sb1("qsb", [128, D], F32)
    ksb = sb1("ksb", [128, D], F32)
    wbuf = [sb1(f"wbuf{j}", [128, D], BF16) for j in range(2)]
    kvbuf = [sb1(f"kvbuf{j}", [128, D], BF16) for j in range(2)]
    s1buf = [sb1(f"s1buf{j}", [128, D], F32) for j in range(2)]
    e1buf = sb1("e1buf", [128, D], F32)
    sqjq = sb1("sqjq", [128, D], F32)
    sqjk = sb1("sqjk", [128, D], F32)
    run_w = sb1("run_w", [128, D], F32)
    run_kv = sb1("run_kv", [128, D], F32)
    bnb = sb1("bnb", [128, 12], F32)
    bnq = sb1("bnq", [128, 12], F32)
    bnk = sb1("bnk", [128, 12], F32)
    mvtmp = sb1("mvtmp", [128, 2], F32)
    mvq = sb1("mvq", [128, 2], F32)
    mvk = sb1("mvk", [128, 2], F32)
    ssc = sb1("ssc", [128, 8 * NT], F32)
    vsc = sb1("vsc", [128, 8 * NT], F32)

    ps_q = e1.enter_context(nc.psum_tensor("ps_q", [128, D], F32))
    ps_k = e1.enter_context(nc.psum_tensor("ps_k", [128, D], F32))
    ps_v = e1.enter_context(nc.psum_tensor("ps_v", [128, D], F32))
    psT = e1.enter_context(nc.psum_tensor("psT", [128, D], BF16))

    e1c = {}

    with nc.Block() as block:

        @block.sync
        def _(sync):
            for j in range(3):
                sync.dma_start(out=xb1[j][:], in_=x_in[j * 128:(j + 1) * 128, :]).then_inc(dX1[j], 16)
            sync.dma_start(out=tri_s[:], in_=tri[:]).then_inc(sem["dWU"], 16)
            sync.dma_start(out=idn_s[:], in_=idn[:]).then_inc(sem["dC"], 16)
            sync.dma_start(out=ones_s[:], in_=ones2[:]).then_inc(sem["dC"], 16)
            sync.dma_start(out=selw_s[:], in_=selw[:]).then_inc(sem["dC"], 16)
            sync.dma_start(out=selk_s[:], in_=selk[:]).then_inc(sem["dC"], 16)
            sync.dma_start(out=e31_s[:], in_=e31[:]).then_inc(sem["dC"], 16)
            sync.dma_start(out=scl_s[:], in_=scl[:]).then_inc(sem["dC"], 16)
            sync.dma_start(out=bias_s[:], in_=bias_bc[:]).then_inc(sem["dC"], 16)
            sync.dma_start(out=eps_s[:], in_=epsc[:]).then_inc(sem["dC"], 16)
            for j in range(8):
                sync.dma_start(out=wsw_b[:, j * 1024:(j + 1) * 1024],
                               in_=fk[j * 128:(j + 1) * 128, :]).then_inc(sem["dFK"], 16)
            for i in range(NT):
                if i + 3 < NT:
                    j = i + 3
                    sync.wait_ge(sem["vA"], j - 2)
                    sync.dma_start(out=xb1[j % 3][:],
                                   in_=x_in[j * 128:(j + 1) * 128, :]).then_inc(dX1[j % 3], 16)
                sync.wait_ge(sem["sWE"], i + 1)
                sync.dma_start(out=sp_w[i * 128:(i + 1) * 128, :],
                               in_=wbuf[i % 2][:]).then_inc(dSP[i % 2], 16)
                sync.wait_ge(sem["vKV"], i + 1)
                sync.dma_start(out=sp_kv[i * 128:(i + 1) * 128, :],
                               in_=kvbuf[i % 2][:]).then_inc(dSP[i % 2], 16)
                sync.wait_ge(sem["sS1"], i + 1)
                sync.dma_start(out=sp_s1[i * 128:(i + 1) * 128, :],
                               in_=s1buf[i % 2][:]).then_inc(dSS[i % 2], 16)

        @block.gpsimd
        def _(gpsimd):
            for dc in range(8):
                if dc >= 1:
                    gpsimd.wait_ge(sem["dWC"], 16 * dc)
                gpsimd.dma_start(out=wqkv_b[:, dc * 3072:(dc + 1) * 3072],
                                 in_=wqkv[dc * 128:(dc + 1) * 128, :]).then_inc(sem["dWC"], 16)
            gpsimd.wait_ge(sem["dWU"], 16)
            gpsimd.dma_start(out=cc_wi.ap(), in_=tri_s[0:2, :]).then_inc(sem["dWI"], 16)
            gpsimd.wait_ge(sem["dWI"], 16)
            gpsimd.collective_compute(
                "AllGather", ALU.bypass,
                replica_groups=[[0, 1, 2, 3], [4, 5, 6, 7]],
                ins=[cc_wi.ap().opt()],
                outs=[cc_wo.ap().opt()],
            ).then_inc(sem["cSem"], 1)
            gpsimd.memset(run_w[:], 0.0).then_inc(sem["chG"], 1)
            gpsimd.memset(run_kv[:], 0.0).then_inc(sem["chG2"], 1)
            for i in range(NT):
                p = i % 2
                gpsimd.wait_ge(sem["sWE"], i + 1)
                gpsimd.wait_ge(sem["chG"], i + 1)
                gpsimd.tensor_add(run_w[:], run_w[:], wbuf[p][:]).then_inc(sem["chG"], 1)
                gpsimd.wait_ge(sem["vKV"], i + 1)
                gpsimd.wait_ge(sem["chG2"], i + 1)
                gpsimd.tensor_add(run_kv[:], run_kv[:], kvbuf[p][:]).then_inc(sem["chG2"], 1)

            gpsimd.wait_ge(sem["vTot"], 2)
            gpsimd.dma_start(out=cc_in.ap()[0:1, :], in_=totw_s[:]).then_inc(sem["dCC"], 16)
            gpsimd.dma_start(out=cc_in.ap()[1:2, :], in_=totk_s[:]).then_inc(sem["dCC"], 16)
            gpsimd.wait_ge(sem["dCC"], 32)
            gpsimd.collective_compute(
                "AllGather", ALU.bypass,
                replica_groups=[[0, 1, 2, 3], [4, 5, 6, 7]],
                ins=[cc_in.ap().opt()],
                outs=[cc_out.ap().opt()],
            ).then_inc(sem["cSem"], 1)

        @block.scalar
        def _(scalar):
            chs = [0]

            def cinc(inst):
                chs[0] += 1
                inst.then_inc(sem["chS"], 1)
                return chs[0]

            scalar.wait_ge(sem["dC"], 16 * 8)



            def S_1(j):
                o = 8 * j
                scalar.wait_ge(sem["vStat"], j + 1)
                c = cinc(scalar.activation(out=ssc[:, o:o + 1], in_=stat_v[:, j:j + 1],
                                           func=ACT.Ln, bias=eps_s[:, 0:1]))
                scalar.wait_ge(sem["chS"], c)
                scalar.activation(out=stat_r[:, j:j + 1], in_=ssc[:, o:o + 1], func=ACT.Exp,
                                  scale=-0.5).then_inc(sem["sStat"], 1)

            def S_2(j):
                o = 8 * j
                scalar.wait_ge(sem["vStat2"], j + 1)
                c = cinc(scalar.activation(out=ssc[:, o + 1:o + 2], in_=vsc[:, o + 1:o + 2],
                                           func=ACT.Ln, bias=eps_s[:, 1:2]))
                scalar.wait_ge(sem["chS"], c)
                scalar.activation(out=ssc[:, o + 2:o + 3], in_=ssc[:, o + 1:o + 2], func=ACT.Exp,
                                  scale=-0.5).then_inc(sem["sStat2"], 1)

            def S_evacA(j):
                scalar.wait_ge(sem["tTr"], j + 1)
                scalar.activation(out=aT[j % 2][:], in_=psT[:], func=ACT.Copy).then_inc(sem["sTe"], 1)

            def S_tail(j):
                p = j % 2
                o = 8 * j
                scalar.wait_ge(sem["tQK"], j + 1)
                if j >= 1:
                    scalar.wait_ge(sem["sWE"], j)
                scalar.activation(out=ksb[:], in_=ps_k[:, :], func=ACT.Copy).then_inc(sem["sEvK"], 1)
                scalar.wait_ge(sem["sEvK"], j + 1)
                c0 = cinc(scalar.activation(out=sqjk[:], in_=ksb[:], func=ACT.Square,
                                            accum_out=ssc[:, o + 3:o + 4]))
                scalar.wait_ge(sem["chS"], c0)
                c4 = cinc(scalar.activation(out=ssc[:, o + 4:o + 5], in_=ssc[:, o + 3:o + 4],
                                            func=ACT.Ln, scale=1.0 / D, bias=eps_s[:, 1:2]))
                scalar.wait_ge(sem["chS"], c4)
                c5 = cinc(scalar.activation(out=ssc[:, o + 5:o + 6], in_=ssc[:, o + 4:o + 5],
                                            func=ACT.Exp, scale=-0.5))
                if j >= 2:
                    scalar.wait_ge(sem["chG"], j + 1)
                    scalar.wait_ge(dSP[p], 16 * 2 * (j // 2))
                scalar.wait_ge(sem["chS"], c5)
                scalar.activation(out=wbuf[p][:], in_=ksb[:], func=ACT.Exp,
                                  scale=ssc[:, o + 5:o + 6]).then_inc(sem["sWE"], 1)
                scalar.wait_ge(sem["vEvQ"], j + 1)
                cq = cinc(scalar.activation(out=sqjq[:], in_=qsb[:], func=ACT.Square,
                                            accum_out=ssc[:, o + 6:o + 7]))
                scalar.wait_ge(sem["chS"], cq)
                c1 = cinc(scalar.activation(out=ssc[:, o + 7:o + 8], in_=ssc[:, o + 6:o + 7],
                                            func=ACT.Ln, scale=1.0 / D, bias=eps_s[:, 1:2]))
                scalar.wait_ge(sem["chS"], c1)
                c3 = cinc(scalar.activation(out=ssc[:, o + 3:o + 4], in_=ssc[:, o + 7:o + 8],
                                            func=ACT.Exp, scale=-0.5))
                scalar.wait_ge(sem["chS"], c3)
                cn = cinc(scalar.activation(out=ssc[:, o + 4:o + 5], in_=ssc[:, o + 3:o + 4],
                                            func=ACT.Copy, scale=-1.0))
                scalar.wait_ge(sem["chS"], cn)
                if j >= 1:
                    scalar.wait_ge(sem["sS1"], j)
                c6 = cinc(scalar.activation(out=e1buf[:], in_=qsb[:], func=ACT.Exp,
                                            scale=ssc[:, o + 4:o + 5]))
                if j >= 2:
                    scalar.wait_ge(dSS[p], 16 * (j // 2))
                scalar.wait_ge(sem["chS"], c6)
                scalar.activation(out=s1buf[p][:], in_=e1buf[:], func=ACT.Ln,
                                  bias=eps_s[:, 2:3]).then_inc(sem["sS1"], 1)
                e1c[j] = c6

            S_1(0); S_2(0); S_1(1); S_2(1); S_1(2)
            S_evacA(0)
            for i in range(NT):
                if i == 6:
                    scalar.wait_ge(sem["vInv"], 3)
                    scalar.wait_ge(sem["dFK"], 16 * 8)
                    for j in range(0, 8, 2):
                        scalar.activation(out=fk_q[:, j * 1024:(j + 1) * 1024],
                                          in_=wsw_b[:, j * 1024:(j + 1) * 1024],
                                          func=ACT.Copy, scale=inv_s[:, 1:2]).then_inc(sem["sFK"], 1)
                if i + 1 < NT:
                    S_evacA(i + 1)
                S_tail(i)
                if i + 3 < NT:
                    S_1(i + 3)
                if i + 2 < NT:
                    S_2(i + 2)

        @block.vector
        def _(vector):
            chv = [0]
            agg = [0]
            bng = [0]
            bng2 = [0]
            aggx = {}
            aggqk = {}

            def vinc(inst):
                chv[0] += 1
                inst.then_inc(sem["chV"], 1)
                return chv[0]

            vector.wait_ge(sem["dC"], 16 * 8)
            vector.reciprocal(inv_s[:, 0:1], scl_s[:, 0:1]).then_inc(sem["vInv"], 1)
            vector.reciprocal(inv_s[:, 1:2], scl_s[:, 1:2]).then_inc(sem["vInv"], 1)
            vector.tensor_tensor(inv_s[:, 2:3], scl_s[:, 0:1], scl_s[:, 1:2],
                                 ALU.mult).then_inc(sem["vInv"], 1)


            def V_1(j):
                vector.wait_ge(dX1[j % 3], 16 * (j // 3 + 1))
                if j >= 1:
                    vector.wait_ge(sem["vAgg"], aggx.get(j - 1, 0))
                vector.bn_stats(bnb[:, 0:6], xb1[j % 3][:, 0:512]).then_inc(sem["vBng"], 1)
                vector.bn_stats(bnb[:, 6:12], xb1[j % 3][:, 512:1024]).then_inc(sem["vBng"], 1)
                bng[0] += 2
                vector.wait_ge(sem["vBng"], bng[0])
                if j >= 1:
                    vector.wait_ge(sem["vStat"], j)
                vector.bn_aggr(mvtmp[:], bnb[:]).then_inc(sem["vAgg"], 1)
                agg[0] += 1
                aggx[j] = agg[0]
                vector.wait_ge(sem["vAgg"], agg[0])
                vector.tensor_copy(stat_m[:, j:j + 1], mvtmp[:, 0:1])
                vector.tensor_copy(stat_v[:, j:j + 1], mvtmp[:, 1:2]).then_inc(sem["vStat"], 1)

            def V_2(j):
                o = 8 * j
                vector.wait_ge(sem["sStat"], j + 1)
                c = vinc(vector.tensor_tensor(vsc[:, o:o + 1], stat_r[:, j:j + 1],
                                              stat_r[:, j:j + 1], ALU.mult))
                vector.wait_ge(sem["chV"], c)
                vector.wait_ge(sem["vStat"], j + 1)
                vector.tensor_tensor(vsc[:, o + 1:o + 2], stat_v[:, j:j + 1], vsc[:, o:o + 1],
                                     ALU.mult).then_inc(sem["vStat2"], 1)

            def V_3(j):
                o = 8 * j
                vector.wait_ge(sem["sStat2"], j + 1)
                c = vinc(vector.tensor_tensor(vsc[:, o + 4:o + 5], stat_r[:, j:j + 1],
                                              ssc[:, o + 2:o + 3], ALU.mult))
                if j >= 2:
                    vector.wait_ge(sem["tTr"], j - 1)
                vector.wait_ge(sem["chV"], c)
                vector.tensor_scalar(abf[j % 2][:], xb1[j % 3][:], stat_m[:, j:j + 1],
                                     vsc[:, o + 4:o + 5], ALU.subtract, ALU.mult).then_inc(sem["vA"], 1)

            def V_tail(j):
                p = j % 2
                vector.wait_ge(sem["tQK"], j + 1)
                if j >= 1:
                    vector.wait_ge(sem["chS"], e1c[j - 1])
                vector.tensor_copy(qsb[:], ps_q[:, :]).then_inc(sem["vEvQ"], 1)
                vector.wait_ge(sem["sWE"], j + 1)
                vector.wait_ge(sem["tV"], j + 1)
                if j >= 2:
                    vector.wait_ge(dSP[p], 16 * 2 * (j // 2))
                    vector.wait_ge(sem["chG2"], j + 1)
                vector.tensor_tensor(kvbuf[p][:], wbuf[p][:], ps_v[:, :],
                                     ALU.mult).then_inc(sem["vKV"], 1)

            V_1(0); V_2(0); V_1(1); V_2(1); V_3(0); V_1(2)
            for i in range(NT):
                if i == 7:
                    vector.wait_ge(sem["vInv"], 2)
                    vector.wait_ge(sem["dFK"], 16 * 8)
                    for j in range(1, 8, 2):
                        vector.tensor_scalar(fk_q[:, j * 1024:(j + 1) * 1024],
                                             wsw_b[:, j * 1024:(j + 1) * 1024],
                                             inv_s[:, 1:2], None, ALU.mult).then_inc(sem["vFK"], 1)
                if i + 1 < NT:
                    V_3(i + 1)
                V_tail(i)
                if i + 3 < NT:
                    V_1(i + 3)
                if i + 2 < NT:
                    V_2(i + 2)
            vector.wait_ge(sem["tTot"], 1)
            vector.tensor_copy(totw_s[:], ps_q[0:1, :]).then_inc(sem["vTot"], 1)
            vector.tensor_copy(totk_s[:], ps_k[0:1, :]).then_inc(sem["vTot"], 1)


        @block.tensor
        def _(tensor):
            for i in range(NT):
                p = i % 2
                tensor.wait_ge(sem["vA"], i + 1)
                if i >= 1:
                    tensor.wait_ge(sem["sTe"], i)
                for dc in range(8):
                    inst = tensor.transpose(psT[:, dc * 128:(dc + 1) * 128],
                                            abf[p][:, dc * 128:(dc + 1) * 128], idn_s[:])
                inst.then_inc(sem["tTr"], 1)
                tensor.wait_ge(sem["sTe"], i + 1)
                if i >= 1:
                    tensor.wait_ge(sem["vEvQ"], i)
                    tensor.wait_ge(sem["sEvK"], i)
                for dc in range(8):
                    if i == 0:
                        tensor.wait_ge(sem["dWC"], 16 * (dc + 1))
                    lhs = aT[p][:, dc * 128:(dc + 1) * 128]
                    st = dc == 0
                    sp = dc == 7
                    base = dc * 3072
                    tensor.matmul(ps_q[:, 0:512], lhs, wqkv_b[:, base + 0:base + 512], start=st, stop=sp)
                    tensor.matmul(ps_q[:, 512:1024], lhs, wqkv_b[:, base + 512:base + 1024], start=st, stop=sp)
                    tensor.matmul(ps_k[:, 0:512], lhs, wqkv_b[:, base + 1024:base + 1536], start=st, stop=sp)
                    inst = tensor.matmul(ps_k[:, 512:1024], lhs, wqkv_b[:, base + 1536:base + 2048], start=st, stop=sp)
                inst.then_inc(sem["tQK"], 1)
                if i >= 1:
                    tensor.wait_ge(sem["vKV"], i)
                for dc in range(8):
                    lhs = aT[p][:, dc * 128:(dc + 1) * 128]
                    st = dc == 0
                    sp = dc == 7
                    base = dc * 3072
                    tensor.matmul(ps_v[:, 0:512], lhs, wqkv_b[:, base + 2048:base + 2560], start=st, stop=sp)
                    inst = tensor.matmul(ps_v[:, 512:1024], lhs, wqkv_b[:, base + 2560:base + 3072], start=st, stop=sp)
                inst.then_inc(sem["tV"], 1)
            tensor.wait_ge(sem["chG"], NT + 1)
            tensor.wait_ge(sem["chG2"], NT + 1)
            tensor.wait_ge(sem["sWE"], NT)
            tensor.wait_ge(sem["vKV"], NT)
            tensor.wait_ge(sem["sS1"], NT)
            tensor.matmul(ps_q[0:1, 0:512], ones_s[:, 0:1], run_w[:, 0:512], start=True, stop=True)
            tensor.matmul(ps_q[0:1, 512:1024], ones_s[:, 0:1], run_w[:, 512:1024], start=True, stop=True)
            tensor.matmul(ps_k[0:1, 0:512], ones_s[:, 0:1], run_kv[:, 0:512], start=True, stop=True)
            inst = tensor.matmul(ps_k[0:1, 512:1024], ones_s[:, 0:1], run_kv[:, 512:1024], start=True, stop=True)
            inst.then_inc(sem["tTot"], 1)

    e1.close()
    eg = ExitStack()
    gbuf = [eg.enter_context(nc.sbuf_tensor(f"gbuf{j}", [128, D], BF16)) for j in range(NT)]
    e2 = ExitStack()
    sb2 = lambda name, shape, dt: e2.enter_context(nc.sbuf_tensor(name, shape, dt))
    wkb = [sb2(f"wkb{j}", [128, 2 * D], BF16) for j in range(2)]
    s1b = [sb2(f"s1b{j}", [128, D], F32) for j in range(2)]
    l2b = [sb2(f"l2b{j}", [128, D], F32) for j in range(2)]
    t2b = sb2("t2b", [128, D], F32)
    Eb = sb2("Eb", [128, D], F32)
    ybuf = [sb2(f"ybuf{j}", [128, D], BF16) for j in range(2)]
    kvc = [sb2(f"kvc{j}", [128, D], F32) for j in range(2)]
    ext32 = [sb2("ext32_0", [32, 2 * D], F32)] * 2

    xb2a = [sb2(f"xb2a{j}", [128, D], F32) for j in range(2)]
    xnb2 = sb2("xnb2", [128, D], BF16)
    xqT2 = [sb2(f"xqT2_{j}", [128, D], FP8) for j in range(2)]
    ybb2 = [sb2(f"ybb2_{j}", [128, D], BF16) for j in range(2)]
    yb2b2 = [sb2(f"yb2b2_{j}", [128, D], BF16) for j in range(4)]
    ebf2 = [sb2(f"ebf2_{j}", [128, D], BF16) for j in range(2)]
    denb2 = [sb2(f"denb2_{j}", [128, D], BF16) for j in range(2)]
    sgc2 = [sb2("sgc2_0", [128, D], F32)] * 2
    sgb2b2 = [sb2("sgb2b2_0", [128, D], BF16)] * 2

    PW = e2.enter_context(nc.psum_tensor("PW", [128, D], F32))
    PK = e2.enter_context(nc.psum_tensor("PK", [128, D], F32))
    psT2 = [e2.enter_context(nc.psum_tensor(f"psT2_{j}", [128, D], BF16)) for j in range(2)]
    OC2 = e2.enter_context(nc.psum_tensor("OC2", [128, D], F32))

    with nc.Block() as block:

        @block.gpsimd
        def _(gpsimd):
            gpsimd.wait_ge(sem["sFK"], 4)
            gpsimd.wait_ge(sem["vFK"], 4)
            for dc in range(8):
                gpsimd.dma_start(out=wsw_b[:, dc * 2048:(dc + 1) * 2048],
                                 in_=wsw[dc * 128:(dc + 1) * 128, :]).then_inc(sem["dWC"], 16)
            for dc in range(8):
                gpsimd.dma_start(out=wo_b[:, dc * 1024:(dc + 1) * 1024],
                                 in_=wo[dc * 128:(dc + 1) * 128, :]).then_inc(sem["dWC"], 16)

        @block.sync
        def _(sync):
            sync.wait_ge(dSP[0], 16 * 16)
            sync.wait_ge(dSP[1], 16 * 16)
            sync.wait_ge(dSS[0], 16 * 8)
            sync.wait_ge(dSS[1], 16 * 8)
            for i in range(2):
                sync.dma_start(out=wkb[i][:, 0:D], in_=sp_w[i * 128:(i + 1) * 128, :]).then_inc(dWV[i], 16)
                sync.dma_start(out=wkb[i][:, D:2 * D], in_=sp_kv[i * 128:(i + 1) * 128, :]).then_inc(dWV[i], 16)
                sync.dma_start(out=s1b[i][:], in_=sp_s1[i * 128:(i + 1) * 128, :]).then_inc(dWV[i], 16)
            sync.wait_ge(sem["cSem"], 2)
            sync.dma_start(out=tcs[:], in_=cc_out.ap()).then_inc(sem["dTC"], 16)
            for i in range(NT):
                p = i % 2
                if i + 2 < NT:
                    sync.wait_ge(sem["tCum"], i + 1)
                    sync.wait_ge(sem["vT2"], i + 1)
                    sync.dma_start(out=wkb[p][:, 0:D], in_=sp_w[(i + 2) * 128:(i + 3) * 128, :]).then_inc(dWV[p], 16)
                    sync.dma_start(out=wkb[p][:, D:2 * D], in_=sp_kv[(i + 2) * 128:(i + 3) * 128, :]).then_inc(dWV[p], 16)
                    sync.dma_start(out=s1b[p][:], in_=sp_s1[(i + 2) * 128:(i + 3) * 128, :]).then_inc(dWV[p], 16)
                sync.wait_ge(sem["vY"], i + 1)
                sync.dma_start(out=sp_y[i * 128:(i + 1) * 128, :], in_=ybuf[p][:]).then_inc(dYS[p], 16)
            sync.wait_ge(dYS[0], 16 * 8)
            sync.wait_ge(dYS[1], 16 * 8)

        @block.tensor
        def _(tensor):
            tensor.wait_ge(sem["sFK"], 4)
            tensor.wait_ge(sem["vFK"], 4)
            for ii in range(NT + 3):
                if ii < NT:
                    tensor.wait_ge(sem["vXN2"], ii + 1)
                    if ii >= 2:
                        tensor.wait_ge(sem["sXq"], ii - 1)
                    for dc in range(8):
                        inst = tensor.transpose(psT2[ii % 2][:, dc * 128:(dc + 1) * 128],
                                                xnb2[:, dc * 128:(dc + 1) * 128], idn_s[:])
                    inst.then_inc(sem["tXn"], 1)
                if ii == 3:
                    tensor.wait_ge(sem["dTC"], 16)
                    tensor.matmul(PW[0:1, 0:512], selw_s[:, :], tcs[:, 0:512], start=True, stop=True)
                    tensor.matmul(PW[0:1, 512:1024], selw_s[:, :], tcs[:, 512:1024], start=True, stop=True)
                    tensor.matmul(PK[0:1, 0:512], selk_s[:, :], tcs[:, 0:512], start=True, stop=True)
                    inst = tensor.matmul(PK[0:1, 512:1024], selk_s[:, :], tcs[:, 512:1024], start=True, stop=True)
                    inst.then_inc(sem["tSel"], 1)
                if ii >= 3:
                    i = ii - 3
                    p = i % 2
                    tensor.wait_ge(dWV[p], 16 * 3 * (i // 2 + 1))
                    if i == 0:
                        tensor.wait_ge(sem["vOff"], 2)
                    if i >= 1:
                        tensor.wait_ge(sem["sL2"], i)
                        tensor.wait_ge(sem["sExW"], i)
                        tensor.wait_ge(sem["vKV2"], i)
                    tensor.matmul(PW[:, 0:512], tri_s[:], wkb[p][:, 0:512], start=True, stop=False)
                    tensor.matmul(PW[:, 512:1024], tri_s[:], wkb[p][:, 512:1024], start=True, stop=False)
                    tensor.matmul(PK[:, 0:512], tri_s[:], wkb[p][:, D:D + 512], start=True, stop=False)
                    tensor.matmul(PK[:, 512:1024], tri_s[:], wkb[p][:, D + 512:2 * D], start=True, stop=False)
                    if i == 0:
                        tensor.matmul(PW[:, 0:512], ones_s[0:1, :], offw_s[0:1, 0:512], start=False, stop=True)
                        tensor.matmul(PW[:, 512:1024], ones_s[0:1, :], offw_s[0:1, 512:1024], start=False, stop=True)
                        tensor.matmul(PK[:, 0:512], ones_s[0:1, :], offk_s[0:1, 0:512], start=False, stop=True)
                        inst = tensor.matmul(PK[:, 512:1024], ones_s[0:1, :], offk_s[0:1, 512:1024], start=False, stop=True)
                    else:
                        q = 1 - p
                        tensor.wait_ge(sem["sExW"], i)
                        tensor.wait_ge(sem["vExK"], i)
                        tensor.matmul(PW[:, 0:512], e31_s[:, :], ext32[q][:, 0:512], start=False, stop=True)
                        tensor.matmul(PW[:, 512:1024], e31_s[:, :], ext32[q][:, 512:1024], start=False, stop=True)
                        tensor.matmul(PK[:, 0:512], e31_s[:, :], ext32[q][:, D:D + 512], start=False, stop=True)
                        inst = tensor.matmul(PK[:, 512:1024], e31_s[:, :], ext32[q][:, D + 512:2 * D], start=False, stop=True)
                    inst.then_inc(sem["tCum"], 1)
                if ii < NT:
                        tensor.wait_ge(sem["sXq"], ii + 1)
                        if ii >= 1:
                            tensor.wait_ge(sem["sFY"], ii)
                        for dc in range(8):
                            lhs = xqT2[ii % 2][:, dc * 128:(dc + 1) * 128]
                            st = dc == 0
                            sp = dc == 7
                            base = dc * 1024
                            tensor.matmul(OC2[:, 0:512], lhs, fk_q[:, base + 0:base + 512], start=st, stop=sp)
                            inst = tensor.matmul(OC2[:, 512:1024], lhs, fk_q[:, base + 512:base + 1024], start=st, stop=sp)
                        inst.then_inc(sem["tF2"], 1)

        @block.scalar
        def _(scalar):
            cs2 = [0]

            def S_head(j):
                scalar.wait_ge(sem["tCum"], j + 1)
                if j >= 2:
                    scalar.wait_ge(sem["vT2"], j - 1)
                scalar.activation(out=l2b[j % 2][:], in_=PW[:, :], func=ACT.Ln,
                                  bias=eps_s[:, 3:4]).then_inc(sem["sL2"], 1)
                scalar.activation(out=ext32[j % 2][:, 0:D], in_=PW[96:128, :],
                                  func=ACT.Copy).then_inc(sem["sExW"], 1)

            def S_E(j):
                scalar.wait_ge(sem["vT2"], j + 1)
                if j >= 1:
                    scalar.wait_ge(sem["vY"], j)
                scalar.activation(out=Eb[:], in_=t2b[:], func=ACT.Exp,
                                  scale=-1.0).then_inc(sem["sE"], 1)

            scalar.wait_ge(dSP[0], 16 * 16)
            scalar.wait_ge(dSP[1], 16 * 16)
            scalar.wait_ge(dSS[0], 16 * 8)
            scalar.wait_ge(dSS[1], 16 * 8)
            scalar.dma_start(out=xb2a[0][:], in_=x_in[0:128, :]).then_inc(dXA[0], 16)
            scalar.dma_start(out=xb2a[1][:], in_=x_in[128:256, :]).then_inc(dXA[1], 16)
            for ii in range(NT + 6):
                if ii + 2 < NT:
                    scalar.wait_ge(sem["vXN2"], ii + 1)
                    scalar.dma_start(out=xb2a[(ii + 2) % 2][:],
                                     in_=x_in[(ii + 2) * 128:(ii + 3) * 128, :]).then_inc(dXA[(ii + 2) % 2], 16)
                if ii < NT:
                    scalar.wait_ge(sem["tXn"], ii + 1)
                    if ii >= 2:
                        scalar.wait_ge(sem["tF2"], ii - 1)
                    scalar.activation(out=xqT2[ii % 2][:], in_=psT2[ii % 2][:], func=ACT.Copy,
                                      scale=inv_s[:, 0:1]).then_inc(sem["sXq"], 1)
                j = ii - 1
                if 0 <= j < NT:
                    scalar.wait_ge(sem["tF2"], j + 1)
                    if j >= 2:
                        scalar.wait_ge(sem["vYB"], j - 1)
                    scalar.activation(out=ybb2[j % 2][:], in_=OC2[:, :], func=ACT.Copy,
                                      scale=inv_s[:, 2:3]).then_inc(sem["sFY"], 1)
                j = ii - 3
                if 0 <= j < NT:
                    scalar.wait_ge(sem["vYB"], j + 1)
                    if j >= 2:
                        scalar.wait_ge(sem["vDN"], j - 1)
                    scalar.activation(out=ebf2[j % 2][:], in_=yb2b2[j % 4][:], func=ACT.Exp,
                                      scale=-1.0).then_inc(sem["sEB"], 1)
                j = ii - 5
                if 0 <= j < NT:
                    scalar.wait_ge(sem["vDN"], j + 1)
                    scalar.wait_ge(sem["cS2"], 2 * j)
                    cs2[0] += 1
                    scalar.activation(out=sgc2[j % 2][:], in_=denb2[j % 2][:], func=ACT.Ln,
                                      bias=0.0).then_inc(sem["cS2"], 1)
                j = ii - 3
                if 0 <= j < NT:
                    S_head(j)
                    S_E(j)
                j = ii - 5
                if 0 <= j < NT:
                    scalar.wait_ge(sem["cS2"], 2 * j + 1)
                    if j >= 1:
                        scalar.wait_ge(sem["vGB"], j)
                    cs2[0] += 1
                    scalar.activation(out=sgb2b2[j % 2][:], in_=sgc2[j % 2][:], func=ACT.Exp,
                                      scale=-1.0).then_inc(sem["cS2"], 1)

        @block.vector
        def _(vector):
            cv2 = [0]

            def V_head(j):
                pj = j % 2
                if j == 0:
                    vector.wait_ge(sem["tSel"], 1)
                    vector.tensor_copy(offw_s[:], PW[0:1, :]).then_inc(sem["vOff"], 1)
                    vector.tensor_copy(offk_s[:], PK[0:1, :]).then_inc(sem["vOff"], 1)
                vector.wait_ge(sem["tCum"], j + 1)
                if j >= 2:
                    vector.wait_ge(sem["vY"], j - 1)
                vector.tensor_copy(kvc[pj][:], PK[:, :]).then_inc(sem["vKV2"], 1)
                vector.wait_ge(sem["vKV2"], j + 1)
                vector.tensor_copy(ext32[pj][:, D:2 * D], kvc[pj][96:128, :]).then_inc(sem["vExK"], 1)
                vector.wait_ge(sem["sL2"], j + 1)
                if j >= 1:
                    vector.wait_ge(sem["sE"], j)
                vector.tensor_tensor(t2b[:], s1b[pj][:], l2b[pj][:], ALU.add).then_inc(sem["vT2"], 1)

            def V_y(j):
                pj = j % 2
                if j >= 2:
                    vector.wait_ge(dYS[pj], 16 * (j // 2))
                vector.wait_ge(sem["sE"], j + 1)
                vector.tensor_tensor(ybuf[pj][:], kvc[pj][:], Eb[:], ALU.mult).then_inc(sem["vY"], 1)

            for ii in range(NT + 6):
                if ii < NT:
                    vector.wait_ge(dXA[ii % 2], 16 * (ii // 2 + 1))
                    if ii >= 1:
                        vector.wait_ge(sem["tXn"], ii)
                    vector.tensor_scalar(xnb2[:], xb2a[ii % 2][:], stat_m[:, ii:ii + 1],
                                         stat_r[:, ii:ii + 1], ALU.subtract,
                                         ALU.mult).then_inc(sem["vXN2"], 1)
                j = ii - 6
                if 0 <= j < NT:
                    vector.wait_ge(sem["cS2"], 2 * j + 2)
                    vector.tensor_tensor(gbuf[j][:], yb2b2[j % 4][:], sgb2b2[j % 2][:],
                                         ALU.mult).then_inc(sem["vGB"], 1)
                j = ii - 4
                if 0 <= j < NT:
                    vector.wait_ge(sem["sEB"], j + 1)
                    if j >= 2:
                        vector.wait_ge(sem["cS2"], 2 * (j - 2) + 1)
                    vector.tensor_scalar(denb2[j % 2][:], ebf2[j % 2][:], 1.0, None,
                                         ALU.add).then_inc(sem["vDN"], 1)
                j = ii - 2
                if 0 <= j < NT:
                    vector.wait_ge(sem["sFY"], j + 1)
                    if j >= 4:
                        vector.wait_ge(sem["vGB"], j - 3)
                    if j >= 2:
                        vector.wait_ge(sem["sEB"], j - 1)
                    vector.tensor_tensor(yb2b2[j % 4][:], ybb2[j % 2][:], bias_s[:],
                                         ALU.add).then_inc(sem["vYB"], 1)
                j = ii - 3
                if 0 <= j < NT:
                    V_head(j)
                    V_y(j)

    e2.close()
    e3 = ExitStack()
    sb3 = lambda name, shape, dt: e3.enter_context(nc.sbuf_tensor(name, shape, dt))
    yb2t = [sb3(f"yb2t{j}", [128, D], BF16) for j in range(2)]
    xb2 = [sb3(f"xb2_{j}", [128, D], F32) for j in range(2)]
    yT_s = [sb3(f"yT_s{j}", [128, D], BF16) for j in range(2)]
    thb = sb3("thb", [128, D], F32)
    sgb = sb3("sgb", [128, D], F32)
    sgb2 = sb3("sgb2", [128, D], F32)
    pb = sb3("pb", [128, D], F32)
    hb = sb3("hb", [128, D], BF16)
    hT_s = sb3("hT_s", [128, D], BF16)
    xnf = sb3("xnf", [128, D], F32)
    sbf = sb3("sbf", [128, D], F32)
    sg3b = sb3("sg3b", [128, D], F32)
    outb = [sb3(f"outb{j}", [128, D], F32) for j in range(2)]

    PT = [e3.enter_context(nc.psum_tensor(f"PT{j}", [128, D], BF16)) for j in range(2)]
    UA = [e3.enter_context(nc.psum_tensor(f"UA{j}", [128, D], F32)) for j in range(2)]
    GB = e3.enter_context(nc.psum_tensor("GB", [128, D], F32))

    with nc.Block() as block:

        @block.sync
        def _(sync):
            for i in range(2):
                sync.dma_start(out=yb2t[i][:], in_=sp_y[i * 128:(i + 1) * 128, :]).then_inc(dYL[i], 16)
                sync.dma_start(out=xb2[i][:], in_=x_in[i * 128:(i + 1) * 128, :]).then_inc(dX2[i], 16)
            for i in range(NT):
                p = i % 2
                if i + 2 < NT:
                    sync.wait_ge(sem["tUG"], i + 1)
                    sync.dma_start(out=yb2t[p][:], in_=sp_y[(i + 2) * 128:(i + 3) * 128, :]).then_inc(dYL[p], 16)
                    sync.wait_ge(sem["vFin"], i + 1)
                    sync.dma_start(out=xb2[p][:], in_=x_in[(i + 2) * 128:(i + 3) * 128, :]).then_inc(dX2[p], 16)
                sync.wait_ge(sem["vFin"], i + 1)
                sync.dma_start(out=out[i * 128:(i + 1) * 128, :], in_=outb[p][:]).then_inc(dOUT[p], 16)
            sync.wait_ge(dOUT[0], 16 * 8)
            sync.wait_ge(dOUT[1], 16 * 8)

        @block.tensor
        def _(tensor):
            tensor.wait_ge(sem["dWC"], 16 * 24)
            for ii in range(NT + 1):
                if ii >= 1:
                    i = ii - 1
                    p = i % 2
                    if i >= 1:
                        tensor.wait_ge(sem["vS"], i)
                    for dc in range(8):
                        lhs = yT_s[i % 2][:, dc * 128:(dc + 1) * 128]
                        st = dc == 0
                        sp = dc == 7
                        base = dc * 2048
                        tensor.matmul(GB[:, 0:512], lhs, wsw_b[:, base + 1024:base + 1536], start=st, stop=sp)
                        inst = tensor.matmul(GB[:, 512:1024], lhs, wsw_b[:, base + 1536:base + 2048], start=st, stop=sp)
                    inst.then_inc(sem["tUG"], 1)
                if ii < NT:
                    i = ii
                    p = i % 2
                    tensor.wait_ge(dYL[p], 16 * (i // 2 + 1))
                    if i >= 1:
                        tensor.wait_ge(sem["sEvY"], i)
                    for dc in range(8):
                        inst = tensor.transpose(PT[0][:, dc * 128:(dc + 1) * 128],
                                                yb2t[p][:, dc * 128:(dc + 1) * 128], idn_s[:])
                    inst.then_inc(sem["tTY"], 1)
                    tensor.wait_ge(sem["sEvY"], i + 1)
                    if i >= 2:
                        tensor.wait_ge(sem["vH"], i - 1)
                    for dc in range(8):
                        lhs = yT_s[i % 2][:, dc * 128:(dc + 1) * 128]
                        st = dc == 0
                        sp = dc == 7
                        base = dc * 2048
                        tensor.matmul(UA[i % 2][:, 0:512], lhs, wsw_b[:, base + 0:base + 512], start=st, stop=sp)
                        inst = tensor.matmul(UA[i % 2][:, 512:1024], lhs, wsw_b[:, base + 512:base + 1024], start=st, stop=sp)
                    inst.then_inc(sem["tUA"], 1)
                if ii >= 1:
                    i = ii - 1
                    p = i % 2
                    tensor.wait_ge(sem["vH"], i + 1)
                    if i >= 1:
                        tensor.wait_ge(sem["sEvH"], i)
                    for dc in range(8):
                        inst = tensor.transpose(PT[1][:, dc * 128:(dc + 1) * 128],
                                                hb[:, dc * 128:(dc + 1) * 128], idn_s[:])
                    inst.then_inc(sem["tTH"], 1)
                    tensor.wait_ge(sem["sEvH"], i + 1)
                    tensor.wait_ge(sem["sSg"], i + 1)
                    for dc in range(8):
                        lhs = hT_s[:, dc * 128:(dc + 1) * 128]
                        st = dc == 0
                        sp = dc == 7
                        base = dc * 1024
                        tensor.matmul(GB[:, 0:512], lhs, wo_b[:, base + 0:base + 512], start=st, stop=sp)
                        inst = tensor.matmul(GB[:, 512:1024], lhs, wo_b[:, base + 512:base + 1024], start=st, stop=sp)
                    inst.then_inc(sem["tO"], 1)

        @block.scalar
        def _(scalar):
            for ii in range(NT + 1):
                if ii < NT:
                    i = ii
                    scalar.wait_ge(sem["tTY"], i + 1)
                    if i >= 2:
                        scalar.wait_ge(sem["tUG"], i - 1)
                    scalar.activation(out=yT_s[i % 2][:], in_=PT[0][:], func=ACT.Copy).then_inc(sem["sEvY"], 1)
                if ii >= 1:
                    i = ii - 1
                    scalar.wait_ge(sem["tUG"], i + 1)
                    scalar.activation(out=thb[:], in_=GB[:, :], func=ACT.Exp,
                                      scale=-1.0).then_inc(sem["chS2"], 1)
                    scalar.wait_ge(sem["chS2"], 2 * i + 1)
                    scalar.activation(out=sgb[:], in_=thb[:], func=ACT.Ln,
                                      bias=eps_s[:, 2:3]).then_inc(sem["chS2"], 1)
                    scalar.wait_ge(sem["chS2"], 2 * i + 2)
                    scalar.activation(out=sgb2[:], in_=sgb[:], func=ACT.Exp,
                                      scale=-1.0).then_inc(sem["sSg"], 1)
                    scalar.wait_ge(sem["tTH"], i + 1)
                    scalar.activation(out=hT_s[:], in_=PT[1][:], func=ACT.Copy).then_inc(sem["sEvH"], 1)

        @block.vector
        def _(vector):
            for i in range(NT):
                p = i % 2
                vector.wait_ge(dX2[p], 16 * (i // 2 + 1))
                if i >= 1:
                    vector.wait_ge(sem["vS"], i)
                vector.tensor_scalar(xnf[:], xb2[p][:], stat_m[:, i:i + 1], stat_r[:, i:i + 1],
                                     ALU.subtract, ALU.mult).then_inc(sem["chV2"], 1)
                vector.wait_ge(sem["sSg"], i + 1)
                if i >= 1:
                    vector.wait_ge(sem["vH"], i)
                vector.tensor_tensor(pb[:], GB[:, :], sgb2[:], ALU.mult).then_inc(sem["chV2"], 1)
                vector.wait_ge(sem["chV2"], 3 * i + 2)
                vector.wait_ge(sem["tUA"], i + 1)
                if i >= 1:
                    vector.wait_ge(sem["tTH"], i)
                vector.tensor_tensor(hb[:], UA[i % 2][:, :], pb[:], ALU.mult).then_inc(sem["vH"], 1)
                vector.wait_ge(sem["tO"], i + 1)
                if i >= 1:
                    vector.wait_ge(sem["vFin"], i)
                vector.tensor_tensor(sbf[:], xnf[:], GB[:, :], ALU.add).then_inc(sem["vS"], 1)
                vector.wait_ge(sem["vS"], i + 1)
                vector.tensor_tensor(sg3b[:], sbf[:], gbuf[i][:], ALU.mult).then_inc(sem["chV2"], 1)
                if i >= 2:
                    vector.wait_ge(dOUT[p], 16 * (i // 2))
                vector.wait_ge(sem["chV2"], 3 * i + 3)
                vector.tensor_tensor(outb[p][:], xb2[p][:], sg3b[:], ALU.add).then_inc(sem["vFin"], 1)

    e3.close()
    eg.close()
    eo.close()
    return nc


_NC_CACHE = {}


def _get_nc():
    if "nc" not in _NC_CACHE:
        _NC_CACHE["nc"] = build_nc()
    return _NC_CACHE["nc"]


def kernel(L0_attn_x, L0_attn_w_qkv, L0_attn_w_swiglu, L0_attn_w_out,
           L0_ffn_kernel, L0_ffn_input_scale, L0_ffn_kernel_scale, L0_ffn_bias):
    x = np.asarray(L0_attn_x, dtype=np.float32)
    wqkv_t = np.ascontiguousarray(np.asarray(L0_attn_w_qkv, dtype=np.float32).T)
    wsw_t = np.ascontiguousarray(np.asarray(L0_attn_w_swiglu, dtype=np.float32).T)
    wo_t = np.ascontiguousarray(np.asarray(L0_attn_w_out, dtype=np.float32).T)
    fk = np.ascontiguousarray(np.asarray(L0_ffn_kernel).astype(ml_dtypes.bfloat16, copy=False))
    bias = np.asarray(L0_ffn_bias).astype(ml_dtypes.bfloat16, copy=False)
    bias_bc = np.ascontiguousarray(np.broadcast_to(bias[None, :], (128, D)))
    s_in = np.asarray(L0_ffn_input_scale, dtype=np.float32).reshape(1)[0]
    s_k = np.asarray(L0_ffn_kernel_scale, dtype=np.float32).reshape(1)[0]
    scl = np.ascontiguousarray(np.broadcast_to(np.array([[s_in, s_k]], dtype=np.float32), (128, 2)))

    tri = np.ascontiguousarray(np.triu(np.ones((128, 128), dtype=np.float32)).astype(ml_dtypes.bfloat16))
    idn = np.ascontiguousarray(np.eye(128, dtype=np.float32).astype(ml_dtypes.bfloat16))
    ones2 = np.ones((128, 128), dtype=np.float32)
    e31_h = np.zeros((32, 128), dtype=np.float32)
    e31_h[31, :] = 1.0
    epsc = np.ascontiguousarray(np.broadcast_to(
        np.array([[LN_EPS, FP32_EPS, 1.0, W_EPS]], dtype=np.float32), (128, 4)))

    in_maps = []
    for c in range(8):
        b, q = c // 4, c % 4
        xs = np.ascontiguousarray(x[b, q * 2048:(q + 1) * 2048, :])
        ohw_h = np.zeros((1, 16), dtype=np.float32)
        ohw_h[0, 2 * c] = 1.0
        ohk_h = np.zeros((1, 16), dtype=np.float32)
        ohk_h[0, 2 * c + 1] = 1.0
        selw_h = np.zeros((8, 1), dtype=np.float32)
        selk_h = np.zeros((8, 1), dtype=np.float32)
        for r in range(q):
            selw_h[2 * r, 0] = 1.0
            selk_h[2 * r + 1, 0] = 1.0
        in_maps.append({
            "x_in": xs, "wqkv": wqkv_t, "wsw": wsw_t, "wo": wo_t, "fk": fk,
            "bias_bc": bias_bc, "scl": scl, "tri": tri, "idn": idn,
            "ones2": ones2, "ohw": ohw_h, "ohk": ohk_h, "selw": selw_h,
            "selk": selk_h, "e31": e31_h, "epsc": epsc,
        })

    nc = _get_nc()
    res = run_bass_kernel_spmd(nc, in_maps, core_ids=list(range(8)))
    out = np.empty((2, 8192, D), dtype=np.float32)
    for c in range(8):
        b, q = c // 4, c % 4
        out[b, q * 2048:(q + 1) * 2048, :] = res.results[c]["out"]
    return out
